# version 30
# baseline (speedup 1.0000x reference)
import numpy as np
import ml_dtypes
import concourse.bass as bass
import concourse.mybir as mybir
from concourse.bass_utils import run_bass_kernel_spmd
from contextlib import ExitStack

F32 = mybir.dt.float32
BF16 = mybir.dt.bfloat16
AF = mybir.ActivationFunctionType
OP = mybir.AluOpType
AX = mybir.AxisListType.X

NCORES = 8
T = 2048
D = 1024
NE = 16
IDIM = 512
TCH = T // 128
DCH = D // 128
ICH = IDIM // 128
NEG = -1.0e30
GAMMA = 32.0
EPS = 1e-5


class Ctr:
    def __init__(self, sem):
        self.sem = sem
        self.n = 0

    def inc(self, k=1):
        self.n += k
        return self.n


def build_nc(dbg=None, taps=False):
    nc = bass.Bass("TRN2", target_bir_lowering=False)

    xt = nc.dram_tensor("xt", [T, D], F32, kind="ExternalInput").ap()
    xf = nc.dram_tensor("xf", [D, T], F32, kind="ExternalInput").ap()
    wqT = nc.dram_tensor("wqT", [D, D], F32, kind="ExternalInput").ap()
    wkT = nc.dram_tensor("wkT", [D, D], F32, kind="ExternalInput").ap()
    wv = nc.dram_tensor("wv", [D, D], F32, kind="ExternalInput").ap()
    gwT = nc.dram_tensor("gwT", [D, NE], F32, kind="ExternalInput").ap()
    ebb = nc.dram_tensor("ebb", [128, TCH * NE], F32, kind="ExternalInput").ap()
    gpT = nc.dram_tensor("gpT", [NE, D, IDIM], F32, kind="ExternalInput").ap()
    upT = nc.dram_tensor("upT", [NE, D, IDIM], F32, kind="ExternalInput").ap()
    dpT = nc.dram_tensor("dpT", [NE, IDIM, D], F32, kind="ExternalInput").ap()
    ident = nc.dram_tensor("ident", [128, 128], F32, kind="ExternalInput").ap()
    identb = nc.dram_tensor("identb", [128, 128], BF16, kind="ExternalInput").ap()
    epsv = nc.dram_tensor("epsv", [128, 1], F32, kind="ExternalInput").ap()
    out = nc.dram_tensor("out", [T, D], F32, kind="ExternalOutput").ap()
    if taps:
        dbg_S = nc.dram_tensor("dbg_S", [128, 8192], BF16, kind="ExternalOutput").ap()
        dbg_G3 = nc.dram_tensor("dbg_G3", [128, 8192], BF16, kind="ExternalOutput").ap()
        dbg_h = nc.dram_tensor("dbg_h", [128, 16384], F32, kind="ExternalOutput").ap()
        dbg_c = nc.dram_tensor("dbg_c", [128, 256], F32, kind="ExternalOutput").ap()
        dbg_s = nc.dram_tensor("dbg_s", [128, 256], F32, kind="ExternalOutput").ap()

    es = ExitStack()

    cur = [((nc.sbuf_base + 63) // 64) * 64]

    def sb(name, shape, dtype, at=None):
        nbytes = shape[1] * (2 if dtype == BF16 else 4)
        if at is None:
            at = cur[0]
            cur[0] = at + ((nbytes + 63) // 64) * 64
        return nc.alloc_sbuf_tensor_at(name, shape, dtype, offset=at)

    def psum(name, shape, dtype):
        return es.enter_context(nc.psum_tensor(name, shape, dtype))

    def sem(name):
        return Ctr(es.enter_context(nc.semaphore(name)))

    with es:
        A = sb("A", [128, 16384], BF16)
        B = sb("B", [128, 16384], BF16)
        y_acc = sb("y_acc", [128, 16384], F32)
        ybase = y_acc.manual_sbuf_range[0]
        Sb = sb("Sal", [128, 8192], BF16, at=ybase)
        Gb = sb("G1al", [128, 8192], BF16, at=ybase + 16384)
        Ew = sb("WVal", [128, 8192], BF16, at=ybase + 32768)
        Cw = sb("G2al", [128, 8192], BF16, at=ybase + 49152)
        W2 = sb("W2", [128, 16384], BF16)
        Dw = sb("G3b", [128, 8192], BF16)
        gw_sb = sb("gw_sb", [128, DCH * NE], BF16)
        eb_sb = sb("eb_sb", [128, TCH * NE], F32)
        id_sb = sb("id_sb", [128, 128], F32)
        idb_sb = sb("idb_sb", [128, 128], BF16)
        eps_sb = sb("eps_sb", [128, 1], F32)
        dum_sb = sb("dum_sb", [128, 1], F32)
        s_all = sb("s_all", [128, TCH * NE], F32)
        c_all = sb("c_all", [128, TCH * NE], F32)
        sq_scr = sb("sq_scr", [128, 1024], BF16)
        sg_scr = sb("sg_scr", [128, 1024], BF16)
        hb_rot = sb("hb_rot", [128, 4096], BF16)
        sc_all = sb("sc_all", [128, 256], F32)
        ps_all = sb("ps_all", [128, 384], F32)
        p8_all = sb("p8_all", [128, 128], F32)
        p8b_all = sb("p8b_all", [128, 128], F32)
        m8_all = sb("m8_all", [128, 128], F32)
        mv_all = sb("mv_all", [128, 128], F32)
        mv2_all = sb("mv2_all", [128, 128], F32)
        gm_all = sb("gm_all", [128, 64], F32)
        gmx_all = sb("gmx_all", [128, 256], F32)
        tmp_all = sb("tmp_all", [128, 256], F32)
        tmp2_all = sb("tmp2_all", [128, 256], F32)
        sel_all = sb("sel_all", [128, 256], F32)
        den_r = sb("den_r", [128, 48], F32)
        st_sum = sb("st_sum", [128, TCH], F32)
        st_ha = sb("st_ha", [128, TCH], F32)
        st_hb = sb("st_hb", [128, TCH], F32)
        st_sq = sb("st_sq", [128, TCH], F32)
        st_nm = sb("st_nm", [128, TCH], F32)
        st_var = sb("st_var", [128, TCH], F32)
        st_std = sb("st_std", [128, TCH], F32)
        st_rs = sb("st_rs", [128, TCH], F32)
        st_a = sb("st_a", [128, TCH], F32)
        s2_sum = sb("s2_sum", [128, TCH], F32)
        s2_ha = sb("s2_ha", [128, TCH], F32)
        s2_hb = sb("s2_hb", [128, TCH], F32)
        s2_sq = sb("s2_sq", [128, TCH], F32)
        s2_nm = sb("s2_nm", [128, TCH], F32)
        s2_var = sb("s2_var", [128, TCH], F32)
        s2_std = sb("s2_std", [128, TCH], F32)
        s2_rs = sb("s2_rs", [128, TCH], F32)
        s2_a = sb("s2_a", [128, TCH], F32)

        ps = [psum(f"ps{i}", [128, 512], F32) for i in range(8)]
        pr2_ = [ps[6], ps[7]]

        dXt = sem("dXt")
        dXf = sem("dXf")
        dW = sem("dW")
        dWv = sem("dWv")
        dGw = sem("dGw")
        dFgu = [sem("dFgu0"), sem("dFgu1")]
        dFdp = [sem("dFdp0"), sem("dFdp1")]
        ds = sem("ds")
        do = sem("do")
        pe = sem("pe")
        dv = sem("dv")
        ac = sem("ac")
        tp = sem("tp")

        ops = {k: [] for k in ("sync", "gpsimd", "tensor", "vector", "scalar")}

        def em(eng, f):
            ops[eng].append(f)

        def w(eng, ctr, val):
            if val > 0:
                em(eng, lambda e, s=ctr.sem, v=val: e.wait_ge(s, v))

        def vstep(build):
            vv = dv.inc()
            em("vector", lambda e, b=build, s=dv.sem: b(e).then_inc(s, 1))
            em("vector", lambda e, s=dv.sem, v=vv: e.wait_ge(s, v))
            return vv

        slot_state = [None] * 8
        slot_rr = [0]
        slot_lim = [6]

        def acquire_slot(eng="tensor"):
            i = slot_rr[0] % slot_lim[0]
            slot_rr[0] += 1
            if slot_state[i] is not None:
                c, v = slot_state[i]
                w(eng, c, v)
            return i

        p8_sent = p8_all[:, :].rearrange("p (j c) -> p j c", c=8)[:, :, 4:8]
        mv_sent = mv_all[:, :].rearrange("p (j c) -> p j c", c=8)[:, :, 2:8]
        mv2_sent = mv2_all[:, :].rearrange("p (j c) -> p j c", c=8)[:, :, 4:8]
        em("vector", lambda e: e.memset(p8_sent, -3.0e30))
        em("vector", lambda e: e.memset(mv_sent, 3.0e30))
        vstep(lambda e: e.memset(mv2_sent, 3.0e30))

        def dma(eng, ctr, outap, inap):
            v = ctr.inc(16)
            em(eng, lambda e, o=outap, i=inap, s=ctr.sem: e.dma_start(out=o, in_=i).then_inc(s, 16))
            return v

        xt_r = xt.rearrange("(j p) d -> p j d", p=128)
        A_r = A[:, :].rearrange("p (j d) -> p j d", d=D)
        dma("gpsimd", dXt, A_r, xt_r)
        wqT_r = wqT.rearrange("(c p) d -> p c d", p=128)
        dma("gpsimd", dW, W2[:, 0:8192].rearrange("p (c d) -> p c d", d=D), wqT_r)
        wkT_r = wkT.rearrange("(c p) d -> p c d", p=128)
        dma("gpsimd", dW, W2[:, 8192:16384].rearrange("p (c d) -> p c d", d=D), wkT_r)
        vdWqk = dW.n
        xf_r = xf.rearrange("(c p) t -> p c t", p=128)
        B_r = B[:, :].rearrange("p (c t) -> p c t", t=T)
        dma("gpsimd", dXf, B_r, xf_r)

        gwT_r = gwT.rearrange("(c p) e -> p c e", p=128)
        gw_r = gw_sb[:, :].rearrange("p (c e) -> p c e", e=NE)
        dma("gpsimd", dGw, gw_r, gwT_r)
        wv_r = wv.rearrange("(c p) d -> p c d", p=128)
        Ew_r = Ew[:, :].rearrange("p (c d) -> p c d", d=D)
        dma("gpsimd", dWv, Ew_r, wv_r)
        vdWv = dWv.n

        dma("sync", ds, id_sb[:, :], ident)
        dma("sync", ds, idb_sb[:, :], identb)
        w("scalar", ds, 64)
        vac0 = ac.inc()
        em("scalar", lambda e, s=ac.sem:
           e.activation(dum_sb[:, 0:1], eps_sb[:, 0:1], AF.Sqrt).then_inc(s, 1))
        dma("sync", ds, eb_sb[:, :], ebb)
        dma("sync", ds, eps_sb[:, :], epsv)
        vds = ds.n

        w("tensor", ds, vds)
        for k_ in range(200):
            si = acquire_slot()
            if k_ == 199:
                vpe = pe.inc()
                em("tensor", lambda e, o=ps[si][:, 0:128], s=pe.sem:
                   e.matmul(o, idb_sb[:, :], idb_sb[:, :], start=True, stop=True).then_inc(s, 1))
                for i_ in range(6):
                    slot_state[i_] = (pe, vpe)
            else:
                em("tensor", lambda e, o=ps[si][:, 0:128]:
                   e.matmul(o, idb_sb[:, :], idb_sb[:, :], start=True, stop=True))

        w("tensor", dXt, 16)
        v_s_tile = {}
        for d1c in range(DCH):
            for d2h in range(2):
                if d2h == 0 and d1c >= 4:
                    continue
                si = acquire_slot()
                pst = ps[si]
                for j in range(TCH):
                    last = j == TCH - 1
                    inst_args = (pst[:, :],
                                 A[:, j * D + d1c * 128: j * D + (d1c + 1) * 128],
                                 A[:, j * D + d2h * 512: j * D + d2h * 512 + 512],
                                 j == 0, last)
                    if last:
                        vpe = pe.inc()
                        em("tensor", lambda e, a=inst_args, s=pe.sem:
                           e.matmul(a[0], a[1], a[2], start=a[3], stop=a[4]).then_inc(s, 1))
                    else:
                        em("tensor", lambda e, a=inst_args:
                           e.matmul(a[0], a[1], a[2], start=a[3], stop=a[4]))
                w("vector", pe, vpe)
                vdv = dv.inc()
                dst = Sb[:, d1c * D + d2h * 512: d1c * D + d2h * 512 + 512]
                em("vector", lambda e, o=dst, i=pst[:, :], s=dv.sem:
                   e.tensor_copy(o, i).then_inc(s, 1))
                slot_state[si] = (dv, vdv)
                v_s_tile[(d1c, d2h)] = vdv

        v_s_direct = dv.n
        w("tensor", dW, vdWqk)
        for bc in range(DCH):
            for ah in range(2):
                si = acquire_slot()
                pst = ps[si]
                for kc in range(DCH):
                    last = kc == DCH - 1
                    a = (pst[:, :],
                         W2[:, 8192 + kc * D + bc * 128: 8192 + kc * D + (bc + 1) * 128],
                         W2[:, kc * D + ah * 512: kc * D + ah * 512 + 512],
                         kc == 0, last)
                    if last:
                        vpe = pe.inc()
                        em("tensor", lambda e, a=a, s=pe.sem:
                           e.matmul(a[0], a[1], a[2], start=a[3], stop=a[4]).then_inc(s, 1))
                    else:
                        em("tensor", lambda e, a=a:
                           e.matmul(a[0], a[1], a[2], start=a[3], stop=a[4]))
                w("vector", pe, vpe)
                vdv = dv.inc()
                dst = Gb[:, bc * D + ah * 512: bc * D + ah * 512 + 512]
                em("vector", lambda e, o=dst, i=pst[:, :], s=dv.sem:
                   e.tensor_copy(o, i).then_inc(s, 1))
                slot_state[si] = (dv, vdv)
        v_g1_mm_done = pe.n
        v_g1_evac = dv.n

        w("tensor", ds, vds)
        w("tensor", dv, v_s_tile[(3, 1)])
        for d1c in range(4, DCH):
            si = acquire_slot()
            pst = ps[si]
            for d2c in range(4):
                last = d2c == 3
                srcb = Sb[:, d2c * D + d1c * 128: d2c * D + (d1c + 1) * 128]
                dstp = pst[:, d2c * 128: (d2c + 1) * 128]
                if last:
                    vpe = pe.inc()
                    em("tensor", lambda e, o=dstp, i=srcb, s=pe.sem:
                       e.matmul(o, i, idb_sb[:, :], start=True, stop=True).then_inc(s, 1))
                else:
                    em("tensor", lambda e, o=dstp, i=srcb:
                       e.matmul(o, i, idb_sb[:, :], start=True, stop=True))
            w("vector", pe, vpe)
            vdv = dv.inc()
            dst = Sb[:, d1c * D: d1c * D + 512]
            em("vector", lambda e, o=dst, i=pst[:, :], s=dv.sem:
               e.tensor_copy(o, i).then_inc(s, 1))
            slot_state[si] = (dv, vdv)

        v_s_evac = dv.n
        if taps:
            w("gpsimd", dv, v_s_evac)
            dma("gpsimd", tp, dbg_S, Sb[:, :])

        v_g2_tile = {}
        for ah in range(2):
            for d2c in range(DCH - 1, -1, -1):
                w("tensor", dv, v_g1_evac if d2c >= 4 else v_s_evac)
                si = acquire_slot()
                pst = ps[si]
                for bc in range(DCH):
                    last = bc == DCH - 1
                    a = (pst[:, :],
                         Sb[:, bc * D + d2c * 128: bc * D + (d2c + 1) * 128],
                         Gb[:, bc * D + ah * 512: bc * D + ah * 512 + 512],
                         bc == 0, last)
                    if last:
                        vpe = pe.inc()
                        em("tensor", lambda e, a=a, s=pe.sem:
                           e.matmul(a[0], a[1], a[2], start=a[3], stop=a[4]).then_inc(s, 1))
                    else:
                        em("tensor", lambda e, a=a:
                           e.matmul(a[0], a[1], a[2], start=a[3], stop=a[4]))
                w("vector", pe, vpe)
                vdv = dv.inc()
                dst = Cw[:, d2c * D + ah * 512: d2c * D + ah * 512 + 512]
                em("vector", lambda e, o=dst, i=pst[:, :], s=dv.sem:
                   e.tensor_copy(o, i).then_inc(s, 1))
                slot_state[si] = (dv, vdv)
                v_g2_tile[(d2c, ah)] = vdv
        v_g2_mm_done = pe.n
        v_g2_evac = dv.n

        wset = [
            (W2[:, 0:4096], W2[:, 4096:8192], W2[:, 8192:12288]),
            (W2[:, 12288:16384], Dw[:, 0:4096], Dw[:, 4096:8192]),
        ]

        v_dFgu = [0, 0]
        v_dFdp = [0, 0]

        def wdma_gu(e_idx):
            p = e_idx % 2
            gdst, udst, _ = wset[p]
            dma("gpsimd", dFgu[p], gdst.rearrange("p (c i) -> p c i", i=IDIM),
                gpT[e_idx].rearrange("(c p) i -> p c i", p=128))
            dma("gpsimd", dFgu[p], udst.rearrange("p (c i) -> p c i", i=IDIM),
                upT[e_idx].rearrange("(c p) i -> p c i", p=128))
            v_dFgu[p] = dFgu[p].n

        def wdma_dp(e_idx):
            p = e_idx % 2
            dma("gpsimd", dFdp[p], wset[p][2].rearrange("p (c d) -> p c d", d=D),
                dpT[e_idx].rearrange("(c p) d -> p c d", p=128))
            v_dFdp[p] = dFdp[p].n

        w("gpsimd", pe, v_g1_mm_done)
        wdma_gu(0)
        wdma_dp(0)
        dma("gpsimd", dFgu[1], wset[1][0].rearrange("p (c i) -> p c i", i=IDIM),
            gpT[1].rearrange("(c p) i -> p c i", p=128))

        w("tensor", dWv, vdWv)
        v_g3_tile = {}
        for dh in range(2):
            for acc_ in range(DCH):
                w("tensor", dv, max(v_g2_tile[(d2c_, acc_ // 4)] for d2c_ in range(DCH)))
                si = acquire_slot()
                pst = ps[si]
                for d2c in range(DCH):
                    last = d2c == DCH - 1
                    a = (pst[:, :],
                         Cw[:, d2c * D + acc_ * 128: d2c * D + (acc_ + 1) * 128],
                         Ew[:, d2c * D + dh * 512: d2c * D + dh * 512 + 512],
                         d2c == 0, last)
                    if last:
                        vpe = pe.inc()
                        em("tensor", lambda e, a=a, s=pe.sem:
                           e.matmul(a[0], a[1], a[2], start=a[3], stop=a[4]).then_inc(s, 1))
                    else:
                        em("tensor", lambda e, a=a:
                           e.matmul(a[0], a[1], a[2], start=a[3], stop=a[4]))
                w("scalar", pe, vpe)
                vac = ac.inc()
                dst = Dw[:, acc_ * D + dh * 512: acc_ * D + dh * 512 + 512]
                em("scalar", lambda e, o=dst, i=pst[:, :], s=ac.sem:
                   e.activation(o, i, AF.Copy, scale=GAMMA).then_inc(s, 1))
                slot_state[si] = (ac, vac)
                v_g3_tile[(acc_, dh)] = vac
        v_g3_mm_done = pe.n
        v_g3_evac = ac.n

        if taps:
            w("sync", ac, v_g3_evac)
            dma("sync", tp, dbg_G3, Dw[:, :])

        v_tp = {}
        v_tp_pe = {}
        v_sig_bank = [0, 0]

        def norm_cast(j):
            yj = y_acc[:, j * D: (j + 1) * D]
            hbj = hb_rot[:, (j % 4) * D: (j % 4 + 1) * D]
            if j % 3 != 1:
                w("vector", dv, v_ln1q[j // 4])
                if j >= 4:
                    w("vector", pe, v_tp_pe[(j - 4, 1)])
                vdv = dv.inc()
                em("vector", lambda e, o=hbj, i=yj, m=st_nm[:, j:j + 1],
                   r=st_rs[:, j:j + 1], s=dv.sem:
                   e.tensor_scalar(o, i, m, r, OP.add, OP.mult).then_inc(s, 1))
                v_norm[j] = (dv, vdv)
            else:
                w("scalar", dv, v_ln1q[j // 4])
                if j >= 4:
                    w("scalar", pe, v_tp_pe[(j - 4, 1)])
                vac = ac.inc()
                em("scalar", lambda e, o=hbj, i=yj, r=st_rs[:, j:j + 1],
                   b=st_a[:, j:j + 1], s=ac.sem:
                   e.activation(o, i, AF.Identity, scale=r, bias=b).then_inc(s, 1))
                v_norm[j] = (ac, vac)

        def norm_fp32(j):
            yj = y_acc[:, j * D: (j + 1) * D]
            if j % 3 != 1:
                c_, v_ = v_norm[j]
                w("vector", c_, v_)
                w("vector", ac, v_sq_q1[j // 4])
                vdv = dv.inc()
                em("vector", lambda e, i=yj, m=st_nm[:, j:j + 1], r=st_rs[:, j:j + 1],
                   s=dv.sem:
                   e.tensor_scalar(i, i, m, r, OP.add, OP.mult).then_inc(s, 1))
                v_hf[j] = (dv, vdv)
            else:
                c_, v_ = v_norm[j]
                w("scalar", c_, v_)
                w("scalar", ac, v_sq_q1[j // 4])
                vac = ac.inc()
                em("scalar", lambda e, i=yj, r=st_rs[:, j:j + 1], b=st_a[:, j:j + 1],
                   s=ac.sem:
                   e.activation(i, i, AF.Identity, scale=r, bias=b).then_inc(s, 1))
                v_hf[j] = (ac, vac)

        def packs(j):
            w("tensor", v_norm[j][0], v_norm[j][1])
            for g in range(2):
                if j == 0:
                    si = 6 + g
                else:
                    si = acquire_slot()
                pst = ps[si]
                for q in range(4):
                    c = g * 4 + q
                    last = q == 3
                    srcp = hb_rot[:, (j % 4) * D + c * 128: (j % 4) * D + (c + 1) * 128]
                    dstp = pst[:, q * 128: (q + 1) * 128]
                    if last:
                        vpe = pe.inc()
                        em("tensor", lambda e, o=dstp, i=srcp, s=pe.sem:
                           e.matmul(o, i, idb_sb[:, :], start=True, stop=True).then_inc(s, 1))
                    else:
                        em("tensor", lambda e, o=dstp, i=srcp:
                           e.matmul(o, i, idb_sb[:, :], start=True, stop=True))
                v_tp_pe[(j, g)] = vpe
                dst = A[:, :].rearrange("p (c t) -> p c t", t=T)[:, g * 4:(g + 1) * 4,
                                                                j * 128:(j + 1) * 128]
                srcv = pst[:, :].rearrange("p (c t) -> p c t", t=128)
                if (2 * j + g) % 2 == 0:
                    w("vector", pe, vpe)
                    w("vector", dv, v_r1[TCH - 1])
                    vv = dv.inc()
                    em("vector", lambda e, o=dst, i=srcv, s=dv.sem:
                       e.tensor_copy(o, i).then_inc(s, 1))
                    slot_state[si] = (dv, vv)
                    v_tp[(j, g)] = (dv, vv)
                else:
                    w("scalar", pe, vpe)
                    w("scalar", dv, v_r1[TCH - 1])
                    vv = ac.inc()
                    em("scalar", lambda e, o=dst, i=srcv, s=ac.sem:
                       e.activation(o, i, AF.Copy).then_inc(s, 1))
                    slot_state[si] = (ac, vv)
                    v_tp[(j, g)] = (ac, vv)

        def routing_mm(j):
            for g in range(2):
                c_, v_ = v_tp[(j, g)]
                w("tensor", c_, v_)
            prj = pr2_[j % 2]
            if v_sig_bank[j % 2]:
                w("tensor", ac, v_sig_bank[j % 2])
            for kc in range(DCH):
                last = kc == DCH - 1
                a = (prj[:, 0:NE],
                     A[:, kc * T + j * 128: kc * T + (j + 1) * 128],
                     gw_sb[:, kc * NE: (kc + 1) * NE],
                     kc == 0, last)
                if last:
                    vpe = pe.inc()
                    em("tensor", lambda e, a=a, s=pe.sem:
                       e.matmul(a[0], a[1], a[2], start=a[3], stop=a[4]).then_inc(s, 1))
                else:
                    em("tensor", lambda e, a=a:
                       e.matmul(a[0], a[1], a[2], start=a[3], stop=a[4]))
            w("scalar", pe, vpe)
            vac = ac.inc()
            em("scalar", lambda e, o=s_all[:, j * NE: (j + 1) * NE], i=prj[:, 0:NE], s=ac.sem:
               e.activation(o, i, AF.Sigmoid).then_inc(s, 1))
            v_sig_j[j] = vac
            v_sig_bank[j % 2] = vac
            slot_state[6 + j % 2] = (ac, vac)
            if j == TCH - 1:
                slot_lim[0] = 8

        w("tensor", dXf, 16)
        v_r1 = [0] * TCH
        v_ln1q = [0] * 4
        v_sq_q1 = [0] * 4
        v_norm = [None] * TCH
        v_hf = [None] * TCH
        v_rout_c = [0] * TCH
        v_sig_j = [0] * TCH
        if taps:
            w("vector", tp, 32)
        for j in range(TCH):
            for dh in range(2):
                w("tensor", ac, max(v_g3_tile[(a_, dh)] for a_ in range(DCH)))
                si = acquire_slot()
                pst = ps[si]
                for acc_ in range(DCH):
                    last = acc_ == DCH - 1
                    a = (pst[:, :],
                         B[:, acc_ * T + j * 128: acc_ * T + (j + 1) * 128],
                         Dw[:, acc_ * D + dh * 512: acc_ * D + dh * 512 + 512],
                         acc_ == 0, last)
                    if last:
                        vpe = pe.inc()
                        em("tensor", lambda e, a=a, s=pe.sem:
                           e.matmul(a[0], a[1], a[2], start=a[3], stop=a[4]).then_inc(s, 1))
                    else:
                        em("tensor", lambda e, a=a:
                           e.matmul(a[0], a[1], a[2], start=a[3], stop=a[4]))
                w("vector", pe, vpe)
                vdv = dv.inc()
                yd = y_acc[:, j * D + dh * 512: j * D + dh * 512 + 512]
                xbs = A[:, j * D + dh * 512: j * D + dh * 512 + 512]
                acc = (st_ha if dh == 0 else st_hb)[:, j:j + 1]
                em("vector", lambda e, o=yd, i0=pst[:, :], i1=xbs, a=acc, s=dv.sem:
                   e.scalar_tensor_tensor(o, i0, 1.0, i1, OP.mult, OP.add,
                                          accum_out=a).then_inc(s, 1))
                slot_state[si] = (dv, vdv)
            v_r1[j] = dv.n
            w("scalar", dv, v_r1[j])
            w("scalar", ac, v_sq_prev if j > 0 else vac0)
            vac = ac.inc()
            v_sq_prev = vac
            em("scalar", lambda e, i=y_acc[:, j * D: (j + 1) * D], a=st_sq[:, j:j + 1], s=ac.sem:
               e.activation(sq_scr[:, :], i, AF.Square, accum_out=a).then_inc(s, 1))
            if j % 4 == 3:
                q = j // 4
                sl = slice(4 * q, 4 * q + 4)
                v_sq_q1[q] = vac
                w("vector", ac, vac)
                vstep(lambda e, sl=sl: e.tensor_add(st_sum[:, sl], st_ha[:, sl], st_hb[:, sl]))
                vstep(lambda e, sl=sl: e.tensor_scalar_mul(st_nm[:, sl], st_sum[:, sl], -1.0 / D))
                vstep(lambda e, sl=sl: e.tensor_scalar_mul(st_var[:, sl], st_sq[:, sl], 1.0 / D))
                vstep(lambda e, sl=sl: e.tensor_mul(st_std[:, sl], st_nm[:, sl], st_nm[:, sl]))
                vdv = vstep(lambda e, sl=sl: e.tensor_sub(st_var[:, sl], st_var[:, sl], st_std[:, sl]))
                if q == 0:
                    w("scalar", ds, vds)
                w("scalar", dv, vdv)
                vac = ac.inc()
                em("scalar", lambda e, sl=sl, s=ac.sem:
                   e.activation(st_std[:, sl], st_var[:, sl], AF.Sqrt,
                                bias=eps_sb[:, 0:1]).then_inc(s, 1))
                w("vector", ac, vac)
                vstep(lambda e, sl=sl: e.reciprocal(st_rs[:, sl], st_std[:, sl]))
                v_ln1q[q] = vstep(lambda e, sl=sl: e.tensor_mul(st_a[:, sl], st_nm[:, sl], st_rs[:, sl]))
                if q == 0:
                    for jj in range(4):
                        norm_cast(jj)
        v_y_mm_done = pe.n

        w("gpsimd", pe, v_y_mm_done)
        if taps:
            w("gpsimd", tp, 32)
        dma("gpsimd", dFgu[1], wset[1][1].rearrange("p (c i) -> p c i", i=IDIM),
            upT[1].rearrange("(c p) i -> p c i", p=128))
        v_dFgu[1] = dFgu[1].n
        dma("gpsimd", dFdp[1], wset[1][2].rearrange("p (c d) -> p c d", d=D),
            dpT[1].rearrange("(c p) d -> p c d", p=128))
        v_dFdp[1] = dFdp[1].n

        v_stt = {}
        sg_state = [None, None]
        gu_gidx = [0]
        v_act = [dict() for _ in range(NE)]
        v_gu_last = [0] * NE
        v_down_last = [0] * NE

        def emit_gu_q(e_i, tt, cb=None):
            p = e_i % 2
            pbase = p * 8192
            gp_sb, up_sb, _ = wset[p]
            if tt == 0:
                w("tensor", dFgu[p], v_dFgu[p])
            if e_i == 0:
                mx = {}
                for j_ in range(4 * tt, 4 * tt + 4):
                    for g_ in range(2):
                        c_, v_ = v_tp[(j_, g_)]
                        mx[id(c_)] = (c_, max(v_, mx.get(id(c_), (c_, 0))[1]))
                for c_, v_ in mx.values():
                    w("tensor", c_, v_)
            for ic in range(ICH):
                siG = acquire_slot()
                pstG = ps[siG]
                for hc in range(DCH):
                    last = hc == DCH - 1
                    a = (pstG[:, :],
                         gp_sb.rearrange("p (c i) -> p c i", i=IDIM)[:, hc,
                                                                    ic * 128:(ic + 1) * 128],
                         A[:, hc * T + tt * 512: hc * T + tt * 512 + 512],
                         hc == 0, last)
                    if last:
                        vpeG = pe.inc()
                        em("tensor", lambda e, a=a, s=pe.sem:
                           e.matmul(a[0], a[1], a[2], start=a[3], stop=a[4]).then_inc(s, 1))
                    else:
                        em("tensor", lambda e, a=a:
                           e.matmul(a[0], a[1], a[2], start=a[3], stop=a[4]))
                siU = acquire_slot()
                pstU = ps[siU]
                for hc in range(DCH):
                    last = hc == DCH - 1
                    a = (pstU[:, :],
                         up_sb.rearrange("p (c i) -> p c i", i=IDIM)[:, hc,
                                                                    ic * 128:(ic + 1) * 128],
                         A[:, hc * T + tt * 512: hc * T + tt * 512 + 512],
                         hc == 0, last)
                    if last:
                        vpeU = pe.inc()
                        em("tensor", lambda e, a=a, s=pe.sem:
                           e.matmul(a[0], a[1], a[2], start=a[3], stop=a[4]).then_inc(s, 1))
                    else:
                        em("tensor", lambda e, a=a:
                           e.matmul(a[0], a[1], a[2], start=a[3], stop=a[4]))
                sgslot = gu_gidx[0] % 2
                if sg_state[sgslot] is not None:
                    w("scalar", dv, sg_state[sgslot])
                w("scalar", pe, vpeG)
                vacS = ac.inc()
                sgv = sg_scr[:, sgslot * 512: sgslot * 512 + 512]
                em("scalar", lambda e, o=sgv, i=pstG[:, :], s=ac.sem:
                   e.activation(o, i, AF.Silu).then_inc(s, 1))
                slot_state[siG] = (ac, vacS)
                w("vector", pe, vpeU)
                w("vector", ac, vacS)
                if tt == 0 and ic == 0:
                    if e_i <= 1:
                        w("vector", pe, v_y_mm_done)
                    else:
                        w("vector", pe, v_down_last[e_i - 2])
                vdv = dv.inc()
                adst = B[:, pbase + ic * T + tt * 512: pbase + ic * T + tt * 512 + 512]
                em("vector", lambda e, o=adst, i0=pstU[:, :], i1=sgv, s=dv.sem:
                   e.tensor_tensor(o, i0, i1, OP.mult).then_inc(s, 1))
                slot_state[siU] = (dv, vdv)
                sg_state[sgslot] = vdv
                v_act[e_i][(tt, ic)] = vdv
                gu_gidx[0] += 1
                if e_i == 0:
                    routing_mm(4 * tt + ic)
                if e_i == 1 and routing_thunks:
                    routing_thunks.pop(0)()
                if cb is not None:
                    cb(ic)
            if tt == 3:
                v_gu_last[e_i] = pe.n
                if e_i + 2 < NE:
                    w("gpsimd", pe, v_gu_last[e_i])
                    wdma_gu(e_i + 2)

        def emit_gu(e_i):
            for tt in range(4):
                emit_gu_q(e_i, tt)

        def emit_tail_merged():
            w("scalar", ac, vac0)
            vac = ac.inc()
            em("scalar", lambda e, s=ac.sem:
               e.activation(dum_sb[:, 0:1], eps_sb[:, 0:1], AF.Sqrt).then_inc(s, 1))
            sq_prev = [vac]
            w("tensor", dFdp[0], v_dFdp[0])
            w("tensor", dFdp[1], v_dFdp[1])
            out_r = out.rearrange("(j p) d -> p j d", p=128)
            y_r = y_acc[:, :].rearrange("p (j d) -> p j d", d=D)
            v_r2 = [None] * TCH
            for j in range(TCH):
                tt = j // 4
                for e_i in (NE - 2, NE - 1):
                    p = e_i % 2
                    pbase = p * 8192
                    dp_sb = wset[p][2]
                    for dh in range(2):
                        si = acquire_slot()
                        pst = ps[si]
                        w("tensor", dv, v_act[e_i][(tt, ICH - 1)])
                        for ic in range(ICH):
                            last = ic == ICH - 1
                            a = (pst[:, :],
                                 B[:, pbase + ic * T + j * 128: pbase + ic * T + (j + 1) * 128],
                                 dp_sb.rearrange("p (c d) -> p c d", d=D)[:, ic,
                                                                         dh * 512: dh * 512 + 512],
                                 ic == 0, last)
                            if last:
                                vpe = pe.inc()
                                em("tensor", lambda e, a=a, s=pe.sem:
                                   e.matmul(a[0], a[1], a[2], start=a[3], stop=a[4]).then_inc(s, 1))
                            else:
                                em("tensor", lambda e, a=a:
                                   e.matmul(a[0], a[1], a[2], start=a[3], stop=a[4]))
                        w("vector", pe, vpe)
                        w("vector", dv, v_stt[(j, dh)])
                        vdv = dv.inc()
                        yd = y_acc[:, j * D + dh * 512: j * D + dh * 512 + 512]
                        cw = c_all[:, j * NE + e_i: j * NE + e_i + 1]
                        if e_i == NE - 1:
                            acc = (s2_ha if dh == 0 else s2_hb)[:, j:j + 1]
                            em("vector", lambda e, o=yd, i0=pst[:, :], c=cw, a=acc, s=dv.sem:
                               e.scalar_tensor_tensor(o, i0, c, o, OP.mult, OP.add,
                                                      accum_out=a).then_inc(s, 1))
                        else:
                            em("vector", lambda e, o=yd, i0=pst[:, :], c=cw, s=dv.sem:
                               e.scalar_tensor_tensor(o, i0, c, o, OP.mult, OP.add).then_inc(s, 1))
                        slot_state[si] = (dv, vdv)
                        v_stt[(j, dh)] = vdv
                w("scalar", dv, v_stt[(j, 1)])
                w("scalar", ac, sq_prev[0])
                vac = ac.inc()
                sq_prev[0] = vac
                em("scalar", lambda e, i=y_acc[:, j * D: (j + 1) * D], a=s2_sq[:, j:j + 1],
                   s=ac.sem:
                   e.activation(sq_scr[:, :], i, AF.Square, accum_out=a).then_inc(s, 1))
                if j % 4 != 3:
                    continue
                q = j // 4
                sl = slice(4 * q, 4 * q + 4)
                v_sq_q = vac
                w("vector", ac, v_sq_q)
                w("vector", dv, v_stt[(j, 1)])
                vstep(lambda e, sl=sl: e.tensor_add(s2_sum[:, sl], s2_ha[:, sl], s2_hb[:, sl]))
                vstep(lambda e, sl=sl: e.tensor_scalar_mul(s2_nm[:, sl], s2_sum[:, sl], -1.0 / D))
                vstep(lambda e, sl=sl: e.tensor_scalar_mul(s2_var[:, sl], s2_sq[:, sl], 1.0 / D))
                vstep(lambda e, sl=sl: e.tensor_mul(s2_std[:, sl], s2_nm[:, sl], s2_nm[:, sl]))
                vdv = vstep(lambda e, sl=sl: e.tensor_sub(s2_var[:, sl], s2_var[:, sl],
                                                          s2_std[:, sl]))
                w("scalar", dv, vdv)
                vac = ac.inc()
                em("scalar", lambda e, sl=sl, s=ac.sem:
                   e.activation(s2_std[:, sl], s2_var[:, sl], AF.Sqrt,
                                bias=eps_sb[:, 0:1]).then_inc(s, 1))
                w("vector", ac, vac)
                vstep(lambda e, sl=sl: e.reciprocal(s2_rs[:, sl], s2_std[:, sl]))
                v_ab2 = vstep(lambda e, sl=sl: e.tensor_mul(s2_a[:, sl], s2_nm[:, sl],
                                                            s2_rs[:, sl]))
                w("scalar", dv, v_ab2)
                w("scalar", ac, v_sq_q)
                for k in range(4 * q, 4 * q + 4):
                    yk = y_acc[:, k * D: (k + 1) * D]
                    if k % 4 == 0:
                        vdv = dv.inc()
                        em("vector", lambda e, i=yk, m=s2_nm[:, k:k + 1],
                           r=s2_rs[:, k:k + 1], s=dv.sem:
                           e.tensor_scalar(i, i, m, r, OP.add, OP.mult).then_inc(s, 1))
                        v_r2[k] = (dv, vdv)
                    else:
                        vac = ac.inc()
                        em("scalar", lambda e, i=yk, r=s2_rs[:, k:k + 1],
                           b=s2_a[:, k:k + 1], s=ac.sem:
                           e.activation(i, i, AF.Identity, scale=r, bias=b).then_inc(s, 1))
                        v_r2[k] = (ac, vac)
                eng = "sync" if q % 2 == 0 else "scalar"
                for k in range(4 * q, 4 * q + 4):
                    w(eng, v_r2[k][0], v_r2[k][1])
                dma(eng, do, out_r[:, 4 * q:4 * q + 4, :], y_r[:, 4 * q:4 * q + 4, :])
            w("sync", do, do.n)


        def emit_down(e_i):
            p = e_i % 2
            pbase = p * 8192
            dp_sb = wset[p][2]
            w("tensor", dFdp[p], v_dFdp[p])
            for j in range(TCH):
                tt = j // 4
                for dh in range(2):
                    si = acquire_slot()
                    pst = ps[si]
                    w("tensor", dv, v_act[e_i][(tt, ICH - 1)])
                    for ic in range(ICH):
                        last = ic == ICH - 1
                        a = (pst[:, :],
                             B[:, pbase + ic * T + j * 128: pbase + ic * T + (j + 1) * 128],
                             dp_sb.rearrange("p (c d) -> p c d", d=D)[:, ic,
                                                                     dh * 512: dh * 512 + 512],
                             ic == 0, last)
                        if last:
                            vpe = pe.inc()
                            em("tensor", lambda e, a=a, s=pe.sem:
                               e.matmul(a[0], a[1], a[2], start=a[3], stop=a[4]).then_inc(s, 1))
                        else:
                            em("tensor", lambda e, a=a:
                               e.matmul(a[0], a[1], a[2], start=a[3], stop=a[4]))
                    w("vector", pe, vpe)
                    if e_i == 0:
                        w("vector", dv, v_rout_c[j])
                        w("vector", v_hf[j][0], v_hf[j][1])
                        if j == 0 and dh == 0:
                            w("vector", dv, v_transp_all)
                            w("vector", ac, v_transp_ac)
                            if taps:
                                w("vector", tp, 80)
                    else:
                        w("vector", dv, v_stt[(j, dh)])
                    vdv = dv.inc()
                    yd = y_acc[:, j * D + dh * 512: j * D + dh * 512 + 512]
                    cw = c_all[:, j * NE + e_i: j * NE + e_i + 1]
                    if e_i == NE - 1:
                        acc = (s2_ha if dh == 0 else s2_hb)[:, j:j + 1]
                        em("vector", lambda e, o=yd, i0=pst[:, :], c=cw, a=acc, s=dv.sem:
                           e.scalar_tensor_tensor(o, i0, c, o, OP.mult, OP.add,
                                                  accum_out=a).then_inc(s, 1))
                    else:
                        em("vector", lambda e, o=yd, i0=pst[:, :], c=cw, s=dv.sem:
                           e.scalar_tensor_tensor(o, i0, c, o, OP.mult, OP.add).then_inc(s, 1))
                    slot_state[si] = (dv, vdv)
                    v_stt[(j, dh)] = vdv
            v_down_last[e_i] = pe.n
            if e_i + 2 < NE:
                w("gpsimd", pe, v_down_last[e_i])
                wdma_dp(e_i + 2)

        w("sync", ds, vds)
        w("tensor", ds, vds)
        w("tensor", dGw, 16)
        for j in range(4):
            packs(j)
            norm_cast(j + 4)

        def _mk_cb(tt):
            if tt >= 3:
                return None

            def cb(ic):
                j = 4 * (tt + 1) + ic
                packs(j)
                if j + 4 < TCH:
                    norm_cast(j + 4)
            return cb

        for tt in range(4):
            emit_gu_q(0, tt, cb=_mk_cb(tt))
            for j in range(4 * tt, 4 * tt + 4):
                norm_fp32(j)

        v_transp_all = dv.n
        v_transp_ac = ac.n

        vfence = [dv.n]

        def vops(builders):
            w("vector", dv, vfence[0])
            for b in builders[:-1]:
                em("vector", b)
            vv = dv.inc()
            em("vector", lambda e, b=builders[-1], s=dv.sem: b(e).then_inc(s, 1))
            vfence[0] = vv
            return vv

        routing_thunks = []
        rt = routing_thunks.append
        rt(lambda: (w("vector", ac, v_sig_j[TCH - 1]), w("vector", ds, vds),
                    vops([lambda e: e.tensor_add(sc_all[:, :], s_all[:, :], eb_sb[:, :])])))
        sc_v = sc_all[:, :].rearrange("p (j g e) -> p e j g", e=4, g=4)
        ps_v = ps_all[:, :].rearrange("p (j q g) -> p q j g", q=6, g=4)
        pairs = [(0, 1), (0, 2), (0, 3), (1, 2), (1, 3), (2, 3)]
        rt(lambda: vops([(lambda e, q=q, i0_=i0_, i1_=i1_:
                          e.tensor_add(ps_v[:, q], sc_v[:, i0_], sc_v[:, i1_]))
                         for q, (i0_, i1_) in enumerate(pairs)]))
        rt(lambda: vops([lambda e: e.reduce_max(
            p8_all[:, :].rearrange("p (j c) -> p j c", c=8)[:, :, 0:4],
            ps_all[:, :].rearrange("p (j q g) -> p j g q", q=6, g=4), axis=AX)]))
        rt(lambda: vops([(lambda e, j=j: e.max(m8_all[:, j * 8:(j + 1) * 8],
                                               p8_all[:, j * 8:(j + 1) * 8]))
                         for j in range(TCH)]))
        rt(lambda: vops([(lambda e, j=j: e.tensor_copy(mv_all[:, j * 8:j * 8 + 2],
                                                       m8_all[:, j * 8:j * 8 + 2]))
                         for j in range(TCH)]))
        rt(lambda: vops([(lambda e, j=j: e.match_replace(p8b_all[:, j * 8:(j + 1) * 8],
                                                         mv_all[:, j * 8:(j + 1) * 8],
                                                         p8_all[:, j * 8:(j + 1) * 8], NEG))
                         for j in range(TCH)]))
        rt(lambda: vops([lambda e: e.tensor_single_scalar(
            gm_all[:, :].rearrange("p (j g) -> p j g", g=4),
            p8b_all[:, :].rearrange("p (j c) -> p j c", c=8)[:, :, 0:4],
            NEG, OP.is_equal)]))
        gmv = gm_all[:, :].rearrange("p (j g) -> p j g", g=4)
        gxv = gmx_all[:, :].rearrange("p (j g e) -> p e j g", g=4, e=4)
        rt(lambda: vops([(lambda e, k=k: e.tensor_copy(gxv[:, k], gmv)) for k in range(4)]))
        rt(lambda: vops([lambda e: e.tensor_mul(tmp_all[:, :], sc_all[:, :], gmx_all[:, :])]))
        rt(lambda: vops([(lambda e, j=j: e.max(m8_all[:, j * 8:(j + 1) * 8],
                                               tmp_all[:, j * 16:(j + 1) * 16]))
                         for j in range(TCH)]))
        rt(lambda: vops([(lambda e, j=j: e.tensor_copy(mv2_all[:, j * 8:j * 8 + 4],
                                                       m8_all[:, j * 8:j * 8 + 4]))
                         for j in range(TCH)]))
        rt(lambda: vops([(lambda e, j=j: e.match_replace(tmp2_all[:, j * 16:(j + 1) * 16],
                                                         mv2_all[:, j * 8:(j + 1) * 8],
                                                         tmp_all[:, j * 16:(j + 1) * 16], NEG))
                         for j in range(TCH)]))
        rt(lambda: vops([lambda e: e.tensor_single_scalar(sel_all[:, :], tmp2_all[:, :],
                                                          NEG, OP.is_equal)]))
        rt(lambda: vops([lambda e: e.tensor_mul(c_all[:, :], s_all[:, :], sel_all[:, :])]))
        rt(lambda: vops([lambda e: e.reduce_sum(
            den_r[:, 0:16], c_all[:, :].rearrange("p (j e) -> p j e", e=NE), axis=AX)]))
        rt(lambda: vops([lambda e: e.tensor_scalar_add(den_r[:, 16:32], den_r[:, 0:16], 1e-20)]))
        rt(lambda: vops([lambda e: e.reciprocal(den_r[:, 32:48], den_r[:, 16:32])]))

        def _rt_final():
            v = vops([(lambda e, j=j: e.tensor_scalar_mul(
                c_all[:, j * NE:(j + 1) * NE], c_all[:, j * NE:(j + 1) * NE],
                den_r[:, 32 + j:33 + j])) for j in range(TCH)])
            for j in range(TCH):
                v_rout_c[j] = v
        rt(_rt_final)

        if taps:
            w("sync", dv, v_transp_all)
            dma("sync", tp, dbg_h, y_acc[:, :])
            dma("sync", tp, dbg_c, c_all[:, :])
            dma("sync", tp, dbg_s, s_all[:, :])

        emit_gu(1)
        while routing_thunks:
            routing_thunks.pop(0)()
        for e_i in range(NE - 2):
            emit_down(e_i)
            if e_i + 2 < NE:
                emit_gu(e_i + 2)
        emit_tail_merged()


        if dbg is not None:
            dbg.update(dict(A=A, B=B, y_acc=y_acc, Sb=Sb, Gb=Gb, Ew=Ew, Cw=Cw,
                            W2=W2, Dw=Dw, gw_sb=gw_sb, eb_sb=eb_sb, s_all=s_all,
                            c_all=c_all, st_sum=st_sum, st_sq=st_sq, st_nm=st_nm,
                            st_rs=st_rs, s2_nm=s2_nm, s2_rs=s2_rs))

        with nc.Block() as block:
            @block.sync
            def _(eng):
                for f in ops["sync"]:
                    f(eng)

            @block.gpsimd
            def _(eng):
                for f in ops["gpsimd"]:
                    f(eng)

            @block.tensor
            def _(eng):
                for f in ops["tensor"]:
                    f(eng)

            @block.vector
            def _(eng):
                for f in ops["vector"]:
                    f(eng)

            @block.scalar
            def _(eng):
                for f in ops["scalar"]:
                    f(eng)

    return nc


def kernel(L0_attn_x, L0_attn_w_q, L0_attn_w_k, L0_attn_w_v,
           L0_ffn_gate_weight, L0_ffn_e_bias, L0_ffn_gate_proj,
           L0_ffn_up_proj, L0_ffn_down_proj):
    x = np.asarray(L0_attn_x, dtype=np.float32)
    wq = np.asarray(L0_attn_w_q, dtype=np.float32)
    wk = np.asarray(L0_attn_w_k, dtype=np.float32)
    wv = np.asarray(L0_attn_w_v, dtype=np.float32)
    gw = np.asarray(L0_ffn_gate_weight, dtype=np.float32)
    eb = np.asarray(L0_ffn_e_bias, dtype=np.float32)
    gp = np.asarray(L0_ffn_gate_proj, dtype=np.float32)
    up = np.asarray(L0_ffn_up_proj, dtype=np.float32)
    dp = np.asarray(L0_ffn_down_proj, dtype=np.float32)

    wqT = np.ascontiguousarray(wq.T)
    wkT = np.ascontiguousarray(wk.T)
    wv_c = np.ascontiguousarray(wv)
    gwT = np.ascontiguousarray(gw.T)
    ebb = np.ascontiguousarray(np.tile(eb[None, :], (128, TCH)))
    gpT = np.ascontiguousarray(gp.transpose(0, 2, 1))
    upT = np.ascontiguousarray(up.transpose(0, 2, 1))
    dpT = np.ascontiguousarray(dp.transpose(0, 2, 1))
    ident = np.eye(128, dtype=np.float32)
    identb = np.eye(128, dtype=ml_dtypes.bfloat16)
    epsvec = np.full((128, 1), EPS, dtype=np.float32)

    nc = build_nc()
    in_maps = []
    for core in range(NCORES):
        xt_c = np.ascontiguousarray(x[core])
        xf_c = np.ascontiguousarray(x[core].T)
        in_maps.append({
            "xt": xt_c, "xf": xf_c,
            "wqT": wqT, "wkT": wkT, "wv": wv_c,
            "gwT": gwT, "ebb": ebb,
            "gpT": gpT, "upT": upT, "dpT": dpT,
            "ident": ident, "identb": identb, "epsv": epsvec,
        })
    res = run_bass_kernel_spmd(nc, in_maps, core_ids=list(range(NCORES)))
    outs = [np.asarray(r["out"], dtype=np.float32) for r in res.results]
    return np.stack(outs, axis=0)


if __name__ == "__main__":
    nc = build_nc()
    print("build OK")






# version 31
# speedup vs baseline: 1.0054x; 1.0054x over previous
import numpy as np
import ml_dtypes
import concourse.bass as bass
import concourse.mybir as mybir
from concourse.bass_utils import run_bass_kernel_spmd
from contextlib import ExitStack

F32 = mybir.dt.float32
BF16 = mybir.dt.bfloat16
AF = mybir.ActivationFunctionType
OP = mybir.AluOpType
AX = mybir.AxisListType.X

NCORES = 8
T = 2048
D = 1024
NE = 16
IDIM = 512
TCH = T // 128
DCH = D // 128
ICH = IDIM // 128
NEG = -1.0e30
GAMMA = 32.0
EPS = 1e-5


class Ctr:
    def __init__(self, sem):
        self.sem = sem
        self.n = 0

    def inc(self, k=1):
        self.n += k
        return self.n


def build_nc(dbg=None, taps=False):
    nc = bass.Bass("TRN2", target_bir_lowering=False)

    xt = nc.dram_tensor("xt", [T, D], F32, kind="ExternalInput").ap()
    xf = nc.dram_tensor("xf", [D, T], F32, kind="ExternalInput").ap()
    wqT = nc.dram_tensor("wqT", [D, D], F32, kind="ExternalInput").ap()
    wkT = nc.dram_tensor("wkT", [D, D], F32, kind="ExternalInput").ap()
    wv = nc.dram_tensor("wv", [D, D], F32, kind="ExternalInput").ap()
    gwT = nc.dram_tensor("gwT", [D, NE], F32, kind="ExternalInput").ap()
    ebb = nc.dram_tensor("ebb", [128, TCH * NE], F32, kind="ExternalInput").ap()
    gpT = nc.dram_tensor("gpT", [NE, D, IDIM], F32, kind="ExternalInput").ap()
    upT = nc.dram_tensor("upT", [NE, D, IDIM], F32, kind="ExternalInput").ap()
    dpT = nc.dram_tensor("dpT", [NE, IDIM, D], F32, kind="ExternalInput").ap()
    ident = nc.dram_tensor("ident", [128, 128], F32, kind="ExternalInput").ap()
    identb = nc.dram_tensor("identb", [128, 128], BF16, kind="ExternalInput").ap()
    epsv = nc.dram_tensor("epsv", [128, 1], F32, kind="ExternalInput").ap()
    out = nc.dram_tensor("out", [T, D], F32, kind="ExternalOutput").ap()
    if taps:
        dbg_S = nc.dram_tensor("dbg_S", [128, 8192], BF16, kind="ExternalOutput").ap()
        dbg_G3 = nc.dram_tensor("dbg_G3", [128, 8192], BF16, kind="ExternalOutput").ap()
        dbg_h = nc.dram_tensor("dbg_h", [128, 16384], F32, kind="ExternalOutput").ap()
        dbg_c = nc.dram_tensor("dbg_c", [128, 256], F32, kind="ExternalOutput").ap()
        dbg_s = nc.dram_tensor("dbg_s", [128, 256], F32, kind="ExternalOutput").ap()

    es = ExitStack()

    cur = [((nc.sbuf_base + 63) // 64) * 64]

    def sb(name, shape, dtype, at=None):
        nbytes = shape[1] * (2 if dtype == BF16 else 4)
        if at is None:
            at = cur[0]
            cur[0] = at + ((nbytes + 63) // 64) * 64
        return nc.alloc_sbuf_tensor_at(name, shape, dtype, offset=at)

    def psum(name, shape, dtype):
        return es.enter_context(nc.psum_tensor(name, shape, dtype))

    def sem(name):
        return Ctr(es.enter_context(nc.semaphore(name)))

    with es:
        A = sb("A", [128, 16384], BF16)
        B = sb("B", [128, 16384], BF16)
        y_acc = sb("y_acc", [128, 16384], F32)
        ybase = y_acc.manual_sbuf_range[0]
        Sb = sb("Sal", [128, 8192], BF16, at=ybase)
        Gb = sb("G1al", [128, 8192], BF16, at=ybase + 16384)
        Ew = sb("WVal", [128, 8192], BF16, at=ybase + 32768)
        Cw = sb("G2al", [128, 8192], BF16, at=ybase + 49152)
        W2 = sb("W2", [128, 16384], BF16)
        Dw = sb("G3b", [128, 8192], BF16)
        gw_sb = sb("gw_sb", [128, DCH * NE], BF16)
        eb_sb = sb("eb_sb", [128, TCH * NE], F32)
        id_sb = sb("id_sb", [128, 128], F32)
        idb_sb = sb("idb_sb", [128, 128], BF16)
        eps_sb = sb("eps_sb", [128, 1], F32)
        dum_sb = sb("dum_sb", [128, 1], F32)
        s_all = sb("s_all", [128, TCH * NE], F32)
        c_all = sb("c_all", [128, TCH * NE], F32)
        sq_scr = sb("sq_scr", [128, 1024], BF16)
        sg_scr = sb("sg_scr", [128, 1024], BF16)
        hb_rot = sb("hb_rot", [128, 4096], BF16)
        sc_all = sb("sc_all", [128, 256], F32)
        ps_all = sb("ps_all", [128, 384], F32)
        p8_all = sb("p8_all", [128, 128], F32)
        p8b_all = sb("p8b_all", [128, 128], F32)
        m8_all = sb("m8_all", [128, 128], F32)
        mv_all = sb("mv_all", [128, 128], F32)
        mv2_all = sb("mv2_all", [128, 128], F32)
        gm_all = sb("gm_all", [128, 64], F32)
        gmx_all = sb("gmx_all", [128, 256], F32)
        tmp_all = sb("tmp_all", [128, 256], F32)
        tmp2_all = sb("tmp2_all", [128, 256], F32)
        sel_all = sb("sel_all", [128, 256], F32)
        den_r = sb("den_r", [128, 48], F32)
        st_sum = sb("st_sum", [128, TCH], F32)
        st_ha = sb("st_ha", [128, TCH], F32)
        st_hb = sb("st_hb", [128, TCH], F32)
        st_sq = sb("st_sq", [128, TCH], F32)
        st_nm = sb("st_nm", [128, TCH], F32)
        st_var = sb("st_var", [128, TCH], F32)
        st_std = sb("st_std", [128, TCH], F32)
        st_rs = sb("st_rs", [128, TCH], F32)
        st_a = sb("st_a", [128, TCH], F32)
        s2_sum = sb("s2_sum", [128, TCH], F32)
        s2_ha = sb("s2_ha", [128, TCH], F32)
        s2_hb = sb("s2_hb", [128, TCH], F32)
        s2_sq = sb("s2_sq", [128, TCH], F32)
        s2_nm = sb("s2_nm", [128, TCH], F32)
        s2_var = sb("s2_var", [128, TCH], F32)
        s2_std = sb("s2_std", [128, TCH], F32)
        s2_rs = sb("s2_rs", [128, TCH], F32)
        s2_a = sb("s2_a", [128, TCH], F32)

        ps = [psum(f"ps{i}", [128, 512], F32) for i in range(8)]
        pr2_ = [ps[6], ps[7]]

        dXt = sem("dXt")
        dXf = sem("dXf")
        dW = sem("dW")
        dWv = sem("dWv")
        dGw = sem("dGw")
        dFgu = [sem("dFgu0"), sem("dFgu1")]
        dFdp = [sem("dFdp0"), sem("dFdp1")]
        ds = sem("ds")
        do = sem("do")
        pe = sem("pe")
        dv = sem("dv")
        ac = sem("ac")
        tp = sem("tp")

        ops = {k: [] for k in ("sync", "gpsimd", "tensor", "vector", "scalar")}

        def em(eng, f):
            ops[eng].append(f)

        def w(eng, ctr, val):
            if val > 0:
                em(eng, lambda e, s=ctr.sem, v=val: e.wait_ge(s, v))

        def vstep(build):
            vv = dv.inc()
            em("vector", lambda e, b=build, s=dv.sem: b(e).then_inc(s, 1))
            em("vector", lambda e, s=dv.sem, v=vv: e.wait_ge(s, v))
            return vv

        slot_state = [None] * 8
        slot_rr = [0]
        slot_lim = [6]

        def acquire_slot(eng="tensor"):
            i = slot_rr[0] % slot_lim[0]
            slot_rr[0] += 1
            if slot_state[i] is not None:
                c, v = slot_state[i]
                w(eng, c, v)
            return i

        p8_sent = p8_all[:, :].rearrange("p (j c) -> p j c", c=8)[:, :, 4:8]
        mv_sent = mv_all[:, :].rearrange("p (j c) -> p j c", c=8)[:, :, 2:8]
        mv2_sent = mv2_all[:, :].rearrange("p (j c) -> p j c", c=8)[:, :, 4:8]
        em("vector", lambda e: e.memset(p8_sent, -3.0e30))
        em("vector", lambda e: e.memset(mv_sent, 3.0e30))
        vstep(lambda e: e.memset(mv2_sent, 3.0e30))

        def dma(eng, ctr, outap, inap):
            v = ctr.inc(16)
            em(eng, lambda e, o=outap, i=inap, s=ctr.sem: e.dma_start(out=o, in_=i).then_inc(s, 16))
            return v

        xt_r = xt.rearrange("(j p) d -> p j d", p=128)
        A_r = A[:, :].rearrange("p (j d) -> p j d", d=D)
        dma("gpsimd", dXt, A_r, xt_r)
        wqT_r = wqT.rearrange("(c p) d -> p c d", p=128)
        dma("gpsimd", dW, W2[:, 0:8192].rearrange("p (c d) -> p c d", d=D), wqT_r)
        wkT_r = wkT.rearrange("(c p) d -> p c d", p=128)
        dma("gpsimd", dW, W2[:, 8192:16384].rearrange("p (c d) -> p c d", d=D), wkT_r)
        vdWqk = dW.n
        xf_r = xf.rearrange("(c p) t -> p c t", p=128)
        B_r = B[:, :].rearrange("p (c t) -> p c t", t=T)
        dma("gpsimd", dXf, B_r, xf_r)

        gwT_r = gwT.rearrange("(c p) e -> p c e", p=128)
        gw_r = gw_sb[:, :].rearrange("p (c e) -> p c e", e=NE)
        dma("gpsimd", dGw, gw_r, gwT_r)
        wv_r = wv.rearrange("(c p) d -> p c d", p=128)
        Ew_r = Ew[:, :].rearrange("p (c d) -> p c d", d=D)
        dma("gpsimd", dWv, Ew_r, wv_r)
        vdWv = dWv.n

        dma("sync", ds, id_sb[:, :], ident)
        dma("sync", ds, idb_sb[:, :], identb)
        w("scalar", ds, 64)
        vac0 = ac.inc()
        em("scalar", lambda e, s=ac.sem:
           e.activation(dum_sb[:, 0:1], eps_sb[:, 0:1], AF.Sqrt).then_inc(s, 1))
        dma("sync", ds, eb_sb[:, :], ebb)
        dma("sync", ds, eps_sb[:, :], epsv)
        vds = ds.n

        w("tensor", ds, vds)
        for k_ in range(200):
            si = acquire_slot()
            if k_ == 199:
                vpe = pe.inc()
                em("tensor", lambda e, o=ps[si][:, 0:128], s=pe.sem:
                   e.matmul(o, idb_sb[:, :], idb_sb[:, :], start=True, stop=True).then_inc(s, 1))
                for i_ in range(6):
                    slot_state[i_] = (pe, vpe)
            else:
                em("tensor", lambda e, o=ps[si][:, 0:128]:
                   e.matmul(o, idb_sb[:, :], idb_sb[:, :], start=True, stop=True))

        w("tensor", dXt, 16)
        v_s_tile = {}
        for d1c in range(DCH):
            for d2h in range(2):
                if d2h == 0 and d1c >= 4:
                    continue
                si = acquire_slot()
                pst = ps[si]
                for j in range(TCH):
                    last = j == TCH - 1
                    inst_args = (pst[:, :],
                                 A[:, j * D + d1c * 128: j * D + (d1c + 1) * 128],
                                 A[:, j * D + d2h * 512: j * D + d2h * 512 + 512],
                                 j == 0, last)
                    if last:
                        vpe = pe.inc()
                        em("tensor", lambda e, a=inst_args, s=pe.sem:
                           e.matmul(a[0], a[1], a[2], start=a[3], stop=a[4]).then_inc(s, 1))
                    else:
                        em("tensor", lambda e, a=inst_args:
                           e.matmul(a[0], a[1], a[2], start=a[3], stop=a[4]))
                w("vector", pe, vpe)
                vdv = dv.inc()
                dst = Sb[:, d1c * D + d2h * 512: d1c * D + d2h * 512 + 512]
                em("vector", lambda e, o=dst, i=pst[:, :], s=dv.sem:
                   e.tensor_copy(o, i).then_inc(s, 1))
                slot_state[si] = (dv, vdv)
                v_s_tile[(d1c, d2h)] = vdv

        v_s_direct = dv.n
        w("tensor", dW, vdWqk)
        for bc in range(DCH):
            for ah in range(2):
                si = acquire_slot()
                pst = ps[si]
                for kc in range(DCH):
                    last = kc == DCH - 1
                    a = (pst[:, :],
                         W2[:, 8192 + kc * D + bc * 128: 8192 + kc * D + (bc + 1) * 128],
                         W2[:, kc * D + ah * 512: kc * D + ah * 512 + 512],
                         kc == 0, last)
                    if last:
                        vpe = pe.inc()
                        em("tensor", lambda e, a=a, s=pe.sem:
                           e.matmul(a[0], a[1], a[2], start=a[3], stop=a[4]).then_inc(s, 1))
                    else:
                        em("tensor", lambda e, a=a:
                           e.matmul(a[0], a[1], a[2], start=a[3], stop=a[4]))
                w("vector", pe, vpe)
                vdv = dv.inc()
                dst = Gb[:, bc * D + ah * 512: bc * D + ah * 512 + 512]
                em("vector", lambda e, o=dst, i=pst[:, :], s=dv.sem:
                   e.tensor_copy(o, i).then_inc(s, 1))
                slot_state[si] = (dv, vdv)
        v_g1_mm_done = pe.n
        v_g1_evac = dv.n

        w("tensor", ds, vds)
        w("tensor", dv, v_s_tile[(3, 1)])
        for d1c in range(4, DCH):
            si = acquire_slot()
            pst = ps[si]
            for d2c in range(4):
                last = d2c == 3
                srcb = Sb[:, d2c * D + d1c * 128: d2c * D + (d1c + 1) * 128]
                dstp = pst[:, d2c * 128: (d2c + 1) * 128]
                if last:
                    vpe = pe.inc()
                    em("tensor", lambda e, o=dstp, i=srcb, s=pe.sem:
                       e.matmul(o, i, idb_sb[:, :], start=True, stop=True).then_inc(s, 1))
                else:
                    em("tensor", lambda e, o=dstp, i=srcb:
                       e.matmul(o, i, idb_sb[:, :], start=True, stop=True))
            w("vector", pe, vpe)
            vdv = dv.inc()
            dst = Sb[:, d1c * D: d1c * D + 512]
            em("vector", lambda e, o=dst, i=pst[:, :], s=dv.sem:
               e.tensor_copy(o, i).then_inc(s, 1))
            slot_state[si] = (dv, vdv)

        v_s_evac = dv.n
        if taps:
            w("gpsimd", dv, v_s_evac)
            dma("gpsimd", tp, dbg_S, Sb[:, :])

        v_g2_tile = {}
        for ah in range(2):
            for d2c in range(DCH - 1, -1, -1):
                w("tensor", dv, v_g1_evac if d2c >= 4 else v_s_evac)
                si = acquire_slot()
                pst = ps[si]
                for bc in range(DCH):
                    last = bc == DCH - 1
                    a = (pst[:, :],
                         Sb[:, bc * D + d2c * 128: bc * D + (d2c + 1) * 128],
                         Gb[:, bc * D + ah * 512: bc * D + ah * 512 + 512],
                         bc == 0, last)
                    if last:
                        vpe = pe.inc()
                        em("tensor", lambda e, a=a, s=pe.sem:
                           e.matmul(a[0], a[1], a[2], start=a[3], stop=a[4]).then_inc(s, 1))
                    else:
                        em("tensor", lambda e, a=a:
                           e.matmul(a[0], a[1], a[2], start=a[3], stop=a[4]))
                w("vector", pe, vpe)
                vdv = dv.inc()
                dst = Cw[:, d2c * D + ah * 512: d2c * D + ah * 512 + 512]
                em("vector", lambda e, o=dst, i=pst[:, :], s=dv.sem:
                   e.tensor_copy(o, i).then_inc(s, 1))
                slot_state[si] = (dv, vdv)
                v_g2_tile[(d2c, ah)] = vdv
        v_g2_mm_done = pe.n
        v_g2_evac = dv.n

        wset = [
            (W2[:, 0:4096], W2[:, 4096:8192], W2[:, 8192:12288]),
            (W2[:, 12288:16384], Dw[:, 0:4096], Dw[:, 4096:8192]),
        ]

        v_dFgu = [0, 0]
        v_dFdp = [0, 0]

        def wdma_gu(e_idx):
            p = e_idx % 2
            gdst, udst, _ = wset[p]
            dma("gpsimd", dFgu[p], gdst.rearrange("p (c i) -> p c i", i=IDIM),
                gpT[e_idx].rearrange("(c p) i -> p c i", p=128))
            dma("gpsimd", dFgu[p], udst.rearrange("p (c i) -> p c i", i=IDIM),
                upT[e_idx].rearrange("(c p) i -> p c i", p=128))
            v_dFgu[p] = dFgu[p].n

        def wdma_dp(e_idx):
            p = e_idx % 2
            dma("gpsimd", dFdp[p], wset[p][2].rearrange("p (c d) -> p c d", d=D),
                dpT[e_idx].rearrange("(c p) d -> p c d", p=128))
            v_dFdp[p] = dFdp[p].n

        w("gpsimd", pe, v_g1_mm_done)
        wdma_gu(0)
        wdma_dp(0)
        dma("gpsimd", dFgu[1], wset[1][0].rearrange("p (c i) -> p c i", i=IDIM),
            gpT[1].rearrange("(c p) i -> p c i", p=128))

        w("tensor", dWv, vdWv)
        v_g3_tile = {}
        for dh in range(2):
            for acc_ in range(DCH):
                w("tensor", dv, max(v_g2_tile[(d2c_, acc_ // 4)] for d2c_ in range(DCH)))
                si = acquire_slot()
                pst = ps[si]
                for d2c in range(DCH):
                    last = d2c == DCH - 1
                    a = (pst[:, :],
                         Cw[:, d2c * D + acc_ * 128: d2c * D + (acc_ + 1) * 128],
                         Ew[:, d2c * D + dh * 512: d2c * D + dh * 512 + 512],
                         d2c == 0, last)
                    if last:
                        vpe = pe.inc()
                        em("tensor", lambda e, a=a, s=pe.sem:
                           e.matmul(a[0], a[1], a[2], start=a[3], stop=a[4]).then_inc(s, 1))
                    else:
                        em("tensor", lambda e, a=a:
                           e.matmul(a[0], a[1], a[2], start=a[3], stop=a[4]))
                w("scalar", pe, vpe)
                vac = ac.inc()
                dst = Dw[:, acc_ * D + dh * 512: acc_ * D + dh * 512 + 512]
                em("scalar", lambda e, o=dst, i=pst[:, :], s=ac.sem:
                   e.activation(o, i, AF.Copy, scale=GAMMA).then_inc(s, 1))
                slot_state[si] = (ac, vac)
                v_g3_tile[(acc_, dh)] = vac
        v_g3_mm_done = pe.n
        v_g3_evac = ac.n

        if taps:
            w("sync", ac, v_g3_evac)
            dma("sync", tp, dbg_G3, Dw[:, :])

        v_tp = {}
        v_tp_pe = {}
        v_sig_bank = [0, 0]

        def norm_cast(j):
            yj = y_acc[:, j * D: (j + 1) * D]
            hbj = hb_rot[:, (j % 4) * D: (j % 4 + 1) * D]
            if j % 3 != 1:
                w("vector", dv, v_ln1q[j // 4])
                if j >= 4:
                    w("vector", pe, v_tp_pe[(j - 4, 1)])
                vdv = dv.inc()
                em("vector", lambda e, o=hbj, i=yj, m=st_nm[:, j:j + 1],
                   r=st_rs[:, j:j + 1], s=dv.sem:
                   e.tensor_scalar(o, i, m, r, OP.add, OP.mult).then_inc(s, 1))
                v_norm[j] = (dv, vdv)
            else:
                w("scalar", dv, v_ln1q[j // 4])
                if j >= 4:
                    w("scalar", pe, v_tp_pe[(j - 4, 1)])
                vac = ac.inc()
                em("scalar", lambda e, o=hbj, i=yj, r=st_rs[:, j:j + 1],
                   b=st_a[:, j:j + 1], s=ac.sem:
                   e.activation(o, i, AF.Identity, scale=r, bias=b).then_inc(s, 1))
                v_norm[j] = (ac, vac)

        def norm_fp32(j):
            yj = y_acc[:, j * D: (j + 1) * D]
            if j % 3 != 1:
                c_, v_ = v_norm[j]
                w("vector", c_, v_)
                w("vector", ac, v_sq_q1[j // 4])
                vdv = dv.inc()
                em("vector", lambda e, i=yj, m=st_nm[:, j:j + 1], r=st_rs[:, j:j + 1],
                   s=dv.sem:
                   e.tensor_scalar(i, i, m, r, OP.add, OP.mult).then_inc(s, 1))
                v_hf[j] = (dv, vdv)
            else:
                c_, v_ = v_norm[j]
                w("scalar", c_, v_)
                w("scalar", ac, v_sq_q1[j // 4])
                vac = ac.inc()
                em("scalar", lambda e, i=yj, r=st_rs[:, j:j + 1], b=st_a[:, j:j + 1],
                   s=ac.sem:
                   e.activation(i, i, AF.Identity, scale=r, bias=b).then_inc(s, 1))
                v_hf[j] = (ac, vac)

        def packs(j):
            w("tensor", v_norm[j][0], v_norm[j][1])
            for g in range(2):
                if j == 0:
                    si = 6 + g
                else:
                    si = acquire_slot()
                pst = ps[si]
                for q in range(4):
                    c = g * 4 + q
                    last = q == 3
                    srcp = hb_rot[:, (j % 4) * D + c * 128: (j % 4) * D + (c + 1) * 128]
                    dstp = pst[:, q * 128: (q + 1) * 128]
                    if last:
                        vpe = pe.inc()
                        em("tensor", lambda e, o=dstp, i=srcp, s=pe.sem:
                           e.matmul(o, i, idb_sb[:, :], start=True, stop=True).then_inc(s, 1))
                    else:
                        em("tensor", lambda e, o=dstp, i=srcp:
                           e.matmul(o, i, idb_sb[:, :], start=True, stop=True))
                v_tp_pe[(j, g)] = vpe
                dst = A[:, :].rearrange("p (c t) -> p c t", t=T)[:, g * 4:(g + 1) * 4,
                                                                j * 128:(j + 1) * 128]
                srcv = pst[:, :].rearrange("p (c t) -> p c t", t=128)
                if (2 * j + g) % 2 == 0:
                    w("vector", pe, vpe)
                    w("vector", dv, v_r1[TCH - 1])
                    vv = dv.inc()
                    em("vector", lambda e, o=dst, i=srcv, s=dv.sem:
                       e.tensor_copy(o, i).then_inc(s, 1))
                    slot_state[si] = (dv, vv)
                    v_tp[(j, g)] = (dv, vv)
                else:
                    w("scalar", pe, vpe)
                    w("scalar", dv, v_r1[TCH - 1])
                    vv = ac.inc()
                    em("scalar", lambda e, o=dst, i=srcv, s=ac.sem:
                       e.activation(o, i, AF.Copy).then_inc(s, 1))
                    slot_state[si] = (ac, vv)
                    v_tp[(j, g)] = (ac, vv)

        def routing_mm(j):
            for g in range(2):
                c_, v_ = v_tp[(j, g)]
                w("tensor", c_, v_)
            prj = pr2_[j % 2]
            if v_sig_bank[j % 2]:
                w("tensor", ac, v_sig_bank[j % 2])
            for kc in range(DCH):
                last = kc == DCH - 1
                a = (prj[:, 0:NE],
                     A[:, kc * T + j * 128: kc * T + (j + 1) * 128],
                     gw_sb[:, kc * NE: (kc + 1) * NE],
                     kc == 0, last)
                if last:
                    vpe = pe.inc()
                    em("tensor", lambda e, a=a, s=pe.sem:
                       e.matmul(a[0], a[1], a[2], start=a[3], stop=a[4]).then_inc(s, 1))
                else:
                    em("tensor", lambda e, a=a:
                       e.matmul(a[0], a[1], a[2], start=a[3], stop=a[4]))
            w("scalar", pe, vpe)
            vac = ac.inc()
            em("scalar", lambda e, o=s_all[:, j * NE: (j + 1) * NE], i=prj[:, 0:NE], s=ac.sem:
               e.activation(o, i, AF.Sigmoid).then_inc(s, 1))
            v_sig_j[j] = vac
            v_sig_bank[j % 2] = vac
            slot_state[6 + j % 2] = (ac, vac)
            if j == TCH - 1:
                slot_lim[0] = 8

        w("tensor", dXf, 16)
        v_r1 = [0] * TCH
        v_ln1q = [0] * 4
        v_sq_q1 = [0] * 4
        v_norm = [None] * TCH
        v_hf = [None] * TCH
        v_rout_c = [0] * TCH
        v_sig_j = [0] * TCH
        if taps:
            w("vector", tp, 32)
        for j in range(TCH):
            for dh in range(2):
                w("tensor", ac, max(v_g3_tile[(a_, dh)] for a_ in range(DCH)))
                si = acquire_slot()
                pst = ps[si]
                for acc_ in range(DCH):
                    last = acc_ == DCH - 1
                    a = (pst[:, :],
                         B[:, acc_ * T + j * 128: acc_ * T + (j + 1) * 128],
                         Dw[:, acc_ * D + dh * 512: acc_ * D + dh * 512 + 512],
                         acc_ == 0, last)
                    if last:
                        vpe = pe.inc()
                        em("tensor", lambda e, a=a, s=pe.sem:
                           e.matmul(a[0], a[1], a[2], start=a[3], stop=a[4]).then_inc(s, 1))
                    else:
                        em("tensor", lambda e, a=a:
                           e.matmul(a[0], a[1], a[2], start=a[3], stop=a[4]))
                w("vector", pe, vpe)
                vdv = dv.inc()
                yd = y_acc[:, j * D + dh * 512: j * D + dh * 512 + 512]
                xbs = A[:, j * D + dh * 512: j * D + dh * 512 + 512]
                acc = (st_ha if dh == 0 else st_hb)[:, j:j + 1]
                em("vector", lambda e, o=yd, i0=pst[:, :], i1=xbs, a=acc, s=dv.sem:
                   e.scalar_tensor_tensor(o, i0, 1.0, i1, OP.mult, OP.add,
                                          accum_out=a).then_inc(s, 1))
                slot_state[si] = (dv, vdv)
            v_r1[j] = dv.n
            w("scalar", dv, v_r1[j])
            w("scalar", ac, v_sq_prev if j > 0 else vac0)
            vac = ac.inc()
            v_sq_prev = vac
            em("scalar", lambda e, i=y_acc[:, j * D: (j + 1) * D], a=st_sq[:, j:j + 1], s=ac.sem:
               e.activation(sq_scr[:, :], i, AF.Square, accum_out=a).then_inc(s, 1))
            if j % 4 == 3:
                q = j // 4
                sl = slice(4 * q, 4 * q + 4)
                v_sq_q1[q] = vac
                w("vector", ac, vac)
                vstep(lambda e, sl=sl: e.tensor_add(st_sum[:, sl], st_ha[:, sl], st_hb[:, sl]))
                vstep(lambda e, sl=sl: e.tensor_scalar_mul(st_nm[:, sl], st_sum[:, sl], -1.0 / D))
                vstep(lambda e, sl=sl: e.tensor_scalar_mul(st_var[:, sl], st_sq[:, sl], 1.0 / D))
                vstep(lambda e, sl=sl: e.tensor_mul(st_std[:, sl], st_nm[:, sl], st_nm[:, sl]))
                vdv = vstep(lambda e, sl=sl: e.tensor_sub(st_var[:, sl], st_var[:, sl], st_std[:, sl]))
                if q == 0:
                    w("scalar", ds, vds)
                w("scalar", dv, vdv)
                vac = ac.inc()
                em("scalar", lambda e, sl=sl, s=ac.sem:
                   e.activation(st_std[:, sl], st_var[:, sl], AF.Sqrt,
                                bias=eps_sb[:, 0:1]).then_inc(s, 1))
                w("vector", ac, vac)
                vstep(lambda e, sl=sl: e.reciprocal(st_rs[:, sl], st_std[:, sl]))
                v_ln1q[q] = vstep(lambda e, sl=sl: e.tensor_mul(st_a[:, sl], st_nm[:, sl], st_rs[:, sl]))
                if q == 0:
                    for jj in range(4):
                        norm_cast(jj)
        v_y_mm_done = pe.n

        w("gpsimd", pe, v_y_mm_done)
        if taps:
            w("gpsimd", tp, 32)
        dma("gpsimd", dFgu[1], wset[1][1].rearrange("p (c i) -> p c i", i=IDIM),
            upT[1].rearrange("(c p) i -> p c i", p=128))
        v_dFgu[1] = dFgu[1].n
        dma("gpsimd", dFdp[1], wset[1][2].rearrange("p (c d) -> p c d", d=D),
            dpT[1].rearrange("(c p) d -> p c d", p=128))
        v_dFdp[1] = dFdp[1].n

        v_stt = {}
        sg_state = [None, None]
        gu_gidx = [0]
        v_act = [dict() for _ in range(NE)]
        v_gu_last = [0] * NE
        v_down_last = [0] * NE

        def emit_gu_q(e_i, tt, cb=None):
            p = e_i % 2
            pbase = p * 8192
            gp_sb, up_sb, _ = wset[p]
            if tt == 0:
                w("tensor", dFgu[p], v_dFgu[p])
            if e_i == 0:
                mx = {}
                for j_ in range(4 * tt, 4 * tt + 4):
                    for g_ in range(2):
                        c_, v_ = v_tp[(j_, g_)]
                        mx[id(c_)] = (c_, max(v_, mx.get(id(c_), (c_, 0))[1]))
                for c_, v_ in mx.values():
                    w("tensor", c_, v_)
            for ic in range(ICH):
                siG = acquire_slot()
                pstG = ps[siG]
                for hc in range(DCH):
                    last = hc == DCH - 1
                    a = (pstG[:, :],
                         gp_sb.rearrange("p (c i) -> p c i", i=IDIM)[:, hc,
                                                                    ic * 128:(ic + 1) * 128],
                         A[:, hc * T + tt * 512: hc * T + tt * 512 + 512],
                         hc == 0, last)
                    if last:
                        vpeG = pe.inc()
                        em("tensor", lambda e, a=a, s=pe.sem:
                           e.matmul(a[0], a[1], a[2], start=a[3], stop=a[4]).then_inc(s, 1))
                    else:
                        em("tensor", lambda e, a=a:
                           e.matmul(a[0], a[1], a[2], start=a[3], stop=a[4]))
                siU = acquire_slot()
                pstU = ps[siU]
                for hc in range(DCH):
                    last = hc == DCH - 1
                    a = (pstU[:, :],
                         up_sb.rearrange("p (c i) -> p c i", i=IDIM)[:, hc,
                                                                    ic * 128:(ic + 1) * 128],
                         A[:, hc * T + tt * 512: hc * T + tt * 512 + 512],
                         hc == 0, last)
                    if last:
                        vpeU = pe.inc()
                        em("tensor", lambda e, a=a, s=pe.sem:
                           e.matmul(a[0], a[1], a[2], start=a[3], stop=a[4]).then_inc(s, 1))
                    else:
                        em("tensor", lambda e, a=a:
                           e.matmul(a[0], a[1], a[2], start=a[3], stop=a[4]))
                sgslot = gu_gidx[0] % 2
                if sg_state[sgslot] is not None:
                    w("scalar", dv, sg_state[sgslot])
                w("scalar", pe, vpeG)
                vacS = ac.inc()
                sgv = sg_scr[:, sgslot * 512: sgslot * 512 + 512]
                em("scalar", lambda e, o=sgv, i=pstG[:, :], s=ac.sem:
                   e.activation(o, i, AF.Silu).then_inc(s, 1))
                slot_state[siG] = (ac, vacS)
                w("vector", pe, vpeU)
                w("vector", ac, vacS)
                if tt == 0 and ic == 0:
                    if e_i <= 1:
                        w("vector", pe, v_y_mm_done)
                    else:
                        w("vector", pe, v_down_last[e_i - 2])
                vdv = dv.inc()
                adst = B[:, pbase + ic * T + tt * 512: pbase + ic * T + tt * 512 + 512]
                em("vector", lambda e, o=adst, i0=pstU[:, :], i1=sgv, s=dv.sem:
                   e.tensor_tensor(o, i0, i1, OP.mult).then_inc(s, 1))
                slot_state[siU] = (dv, vdv)
                sg_state[sgslot] = vdv
                v_act[e_i][(tt, ic)] = vdv
                gu_gidx[0] += 1
                if e_i == 0:
                    routing_mm(4 * tt + ic)
                if e_i == 1 and routing_thunks:
                    routing_thunks.pop(0)()
                if cb is not None:
                    cb(ic)
            if tt == 3:
                v_gu_last[e_i] = pe.n
                if e_i + 2 < NE:
                    w("gpsimd", pe, v_gu_last[e_i])
                    wdma_gu(e_i + 2)

        def emit_gu(e_i):
            for tt in range(4):
                emit_gu_q(e_i, tt)

        def emit_tail_merged():
            w("scalar", ac, vac0)
            vac = ac.inc()
            em("scalar", lambda e, s=ac.sem:
               e.activation(dum_sb[:, 0:1], eps_sb[:, 0:1], AF.Sqrt).then_inc(s, 1))
            sq_prev = [vac]
            w("tensor", dFdp[0], v_dFdp[0])
            w("tensor", dFdp[1], v_dFdp[1])
            out_r = out.rearrange("(j p) d -> p j d", p=128)
            y_r = y_acc[:, :].rearrange("p (j d) -> p j d", d=D)
            v_r2 = [None] * TCH
            for j in range(TCH):
                tt = j // 4
                for e_i in (NE - 2, NE - 1):
                    p = e_i % 2
                    pbase = p * 8192
                    dp_sb = wset[p][2]
                    for dh in range(2):
                        si = acquire_slot()
                        pst = ps[si]
                        w("tensor", dv, v_act[e_i][(tt, ICH - 1)])
                        for ic in range(ICH):
                            last = ic == ICH - 1
                            a = (pst[:, :],
                                 B[:, pbase + ic * T + j * 128: pbase + ic * T + (j + 1) * 128],
                                 dp_sb.rearrange("p (c d) -> p c d", d=D)[:, ic,
                                                                         dh * 512: dh * 512 + 512],
                                 ic == 0, last)
                            if last:
                                vpe = pe.inc()
                                em("tensor", lambda e, a=a, s=pe.sem:
                                   e.matmul(a[0], a[1], a[2], start=a[3], stop=a[4]).then_inc(s, 1))
                            else:
                                em("tensor", lambda e, a=a:
                                   e.matmul(a[0], a[1], a[2], start=a[3], stop=a[4]))
                        w("vector", pe, vpe)
                        w("vector", dv, v_stt[(j, dh)])
                        vdv = dv.inc()
                        yd = y_acc[:, j * D + dh * 512: j * D + dh * 512 + 512]
                        cw = c_all[:, j * NE + e_i: j * NE + e_i + 1]
                        if e_i == NE - 1:
                            acc = (s2_ha if dh == 0 else s2_hb)[:, j:j + 1]
                            em("vector", lambda e, o=yd, i0=pst[:, :], c=cw, a=acc, s=dv.sem:
                               e.scalar_tensor_tensor(o, i0, c, o, OP.mult, OP.add,
                                                      accum_out=a).then_inc(s, 1))
                        else:
                            em("vector", lambda e, o=yd, i0=pst[:, :], c=cw, s=dv.sem:
                               e.scalar_tensor_tensor(o, i0, c, o, OP.mult, OP.add).then_inc(s, 1))
                        slot_state[si] = (dv, vdv)
                        v_stt[(j, dh)] = vdv
                w("scalar", dv, v_stt[(j, 1)])
                w("scalar", ac, sq_prev[0])
                vac = ac.inc()
                sq_prev[0] = vac
                em("scalar", lambda e, i=y_acc[:, j * D: (j + 1) * D], a=s2_sq[:, j:j + 1],
                   s=ac.sem:
                   e.activation(sq_scr[:, :], i, AF.Square, accum_out=a).then_inc(s, 1))
                if j % 4 != 3:
                    continue
                q = j // 4
                sl = slice(4 * q, 4 * q + 4)
                v_sq_q = vac
                w("vector", ac, v_sq_q)
                w("vector", dv, v_stt[(j, 1)])
                vstep(lambda e, sl=sl: e.tensor_add(s2_sum[:, sl], s2_ha[:, sl], s2_hb[:, sl]))
                vstep(lambda e, sl=sl: e.tensor_scalar_mul(s2_nm[:, sl], s2_sum[:, sl], -1.0 / D))
                vstep(lambda e, sl=sl: e.tensor_scalar_mul(s2_var[:, sl], s2_sq[:, sl], 1.0 / D))
                vstep(lambda e, sl=sl: e.tensor_mul(s2_std[:, sl], s2_nm[:, sl], s2_nm[:, sl]))
                vdv = vstep(lambda e, sl=sl: e.tensor_sub(s2_var[:, sl], s2_var[:, sl],
                                                          s2_std[:, sl]))
                w("scalar", dv, vdv)
                vac = ac.inc()
                em("scalar", lambda e, sl=sl, s=ac.sem:
                   e.activation(s2_std[:, sl], s2_var[:, sl], AF.Sqrt,
                                bias=eps_sb[:, 0:1]).then_inc(s, 1))
                w("vector", ac, vac)
                vstep(lambda e, sl=sl: e.reciprocal(s2_rs[:, sl], s2_std[:, sl]))
                v_ab2 = vstep(lambda e, sl=sl: e.tensor_mul(s2_a[:, sl], s2_nm[:, sl],
                                                            s2_rs[:, sl]))
                w("scalar", dv, v_ab2)
                w("scalar", ac, v_sq_q)
                for k in range(4 * q, 4 * q + 4):
                    yk = y_acc[:, k * D: (k + 1) * D]
                    if k % 4 == 0:
                        vdv = dv.inc()
                        em("vector", lambda e, i=yk, m=s2_nm[:, k:k + 1],
                           r=s2_rs[:, k:k + 1], s=dv.sem:
                           e.tensor_scalar(i, i, m, r, OP.add, OP.mult).then_inc(s, 1))
                        v_r2[k] = (dv, vdv)
                    else:
                        vac = ac.inc()
                        em("scalar", lambda e, i=yk, r=s2_rs[:, k:k + 1],
                           b=s2_a[:, k:k + 1], s=ac.sem:
                           e.activation(i, i, AF.Identity, scale=r, bias=b).then_inc(s, 1))
                        v_r2[k] = (ac, vac)
                eng = "sync" if q % 2 == 0 else "scalar"
                for k in range(4 * q, 4 * q + 4):
                    w(eng, v_r2[k][0], v_r2[k][1])
                dma(eng, do, out_r[:, 4 * q:4 * q + 4, :], y_r[:, 4 * q:4 * q + 4, :])
            w("sync", do, do.n)


        def emit_down(e_i):
            p = e_i % 2
            pbase = p * 8192
            dp_sb = wset[p][2]
            w("tensor", dFdp[p], v_dFdp[p])
            for j in range(TCH):
                tt = j // 4
                for dh in range(2):
                    si = acquire_slot()
                    pst = ps[si]
                    w("tensor", dv, v_act[e_i][(tt, ICH - 1)])
                    for ic in range(ICH):
                        last = ic == ICH - 1
                        a = (pst[:, :],
                             B[:, pbase + ic * T + j * 128: pbase + ic * T + (j + 1) * 128],
                             dp_sb.rearrange("p (c d) -> p c d", d=D)[:, ic,
                                                                     dh * 512: dh * 512 + 512],
                             ic == 0, last)
                        if last:
                            vpe = pe.inc()
                            em("tensor", lambda e, a=a, s=pe.sem:
                               e.matmul(a[0], a[1], a[2], start=a[3], stop=a[4]).then_inc(s, 1))
                        else:
                            em("tensor", lambda e, a=a:
                               e.matmul(a[0], a[1], a[2], start=a[3], stop=a[4]))
                    w("vector", pe, vpe)
                    if e_i == 0:
                        w("vector", dv, v_rout_c[j])
                        w("vector", v_hf[j][0], v_hf[j][1])
                        if j == 0 and dh == 0:
                            w("vector", dv, v_transp_all)
                            w("vector", ac, v_transp_ac)
                            if taps:
                                w("vector", tp, 80)
                    else:
                        w("vector", dv, v_stt[(j, dh)])
                    vdv = dv.inc()
                    yd = y_acc[:, j * D + dh * 512: j * D + dh * 512 + 512]
                    cw = c_all[:, j * NE + e_i: j * NE + e_i + 1]
                    if e_i == NE - 1:
                        acc = (s2_ha if dh == 0 else s2_hb)[:, j:j + 1]
                        em("vector", lambda e, o=yd, i0=pst[:, :], c=cw, a=acc, s=dv.sem:
                           e.scalar_tensor_tensor(o, i0, c, o, OP.mult, OP.add,
                                                  accum_out=a).then_inc(s, 1))
                    else:
                        em("vector", lambda e, o=yd, i0=pst[:, :], c=cw, s=dv.sem:
                           e.scalar_tensor_tensor(o, i0, c, o, OP.mult, OP.add).then_inc(s, 1))
                    slot_state[si] = (dv, vdv)
                    v_stt[(j, dh)] = vdv
            v_down_last[e_i] = pe.n
            if e_i + 2 < NE:
                w("gpsimd", pe, v_down_last[e_i])
                wdma_dp(e_i + 2)

        w("sync", ds, vds)
        w("tensor", ds, vds)
        w("tensor", dGw, 16)
        for j in range(4):
            packs(j)
            norm_cast(j + 4)

        def _mk_cb(tt):
            if tt >= 3:
                return None
            nxt = [4 * (tt + 1) + i_ for i_ in range(4)]

            def cb(ic):
                for j in ([nxt[0], nxt[1]] if ic == 0 else
                          [nxt[2]] if ic == 1 else
                          [nxt[3]] if ic == 2 else []):
                    packs(j)
                    if j + 4 < TCH:
                        norm_cast(j + 4)
            return cb

        for tt in range(4):
            emit_gu_q(0, tt, cb=_mk_cb(tt))
            for j in range(4 * tt, 4 * tt + 4):
                norm_fp32(j)

        v_transp_all = dv.n
        v_transp_ac = ac.n

        vfence = [dv.n]

        def vops(builders):
            w("vector", dv, vfence[0])
            for b in builders[:-1]:
                em("vector", b)
            vv = dv.inc()
            em("vector", lambda e, b=builders[-1], s=dv.sem: b(e).then_inc(s, 1))
            vfence[0] = vv
            return vv

        routing_thunks = []
        rt = routing_thunks.append
        rt(lambda: (w("vector", ac, v_sig_j[TCH - 1]), w("vector", ds, vds),
                    vops([lambda e: e.tensor_add(sc_all[:, :], s_all[:, :], eb_sb[:, :])])))
        sc_v = sc_all[:, :].rearrange("p (j g e) -> p e j g", e=4, g=4)
        ps_v = ps_all[:, :].rearrange("p (j q g) -> p q j g", q=6, g=4)
        pairs = [(0, 1), (0, 2), (0, 3), (1, 2), (1, 3), (2, 3)]
        rt(lambda: vops([(lambda e, q=q, i0_=i0_, i1_=i1_:
                          e.tensor_add(ps_v[:, q], sc_v[:, i0_], sc_v[:, i1_]))
                         for q, (i0_, i1_) in enumerate(pairs)]))
        rt(lambda: vops([lambda e: e.reduce_max(
            p8_all[:, :].rearrange("p (j c) -> p j c", c=8)[:, :, 0:4],
            ps_all[:, :].rearrange("p (j q g) -> p j g q", q=6, g=4), axis=AX)]))
        rt(lambda: vops([(lambda e, j=j: e.max(m8_all[:, j * 8:(j + 1) * 8],
                                               p8_all[:, j * 8:(j + 1) * 8]))
                         for j in range(TCH)]))
        rt(lambda: vops([(lambda e, j=j: e.tensor_copy(mv_all[:, j * 8:j * 8 + 2],
                                                       m8_all[:, j * 8:j * 8 + 2]))
                         for j in range(TCH)]))
        rt(lambda: vops([(lambda e, j=j: e.match_replace(p8b_all[:, j * 8:(j + 1) * 8],
                                                         mv_all[:, j * 8:(j + 1) * 8],
                                                         p8_all[:, j * 8:(j + 1) * 8], NEG))
                         for j in range(TCH)]))
        rt(lambda: vops([lambda e: e.tensor_single_scalar(
            gm_all[:, :].rearrange("p (j g) -> p j g", g=4),
            p8b_all[:, :].rearrange("p (j c) -> p j c", c=8)[:, :, 0:4],
            NEG, OP.is_equal)]))
        gmv = gm_all[:, :].rearrange("p (j g) -> p j g", g=4)
        gxv = gmx_all[:, :].rearrange("p (j g e) -> p e j g", g=4, e=4)
        rt(lambda: vops([(lambda e, k=k: e.tensor_copy(gxv[:, k], gmv)) for k in range(4)]))
        rt(lambda: vops([lambda e: e.tensor_mul(tmp_all[:, :], sc_all[:, :], gmx_all[:, :])]))
        rt(lambda: vops([(lambda e, j=j: e.max(m8_all[:, j * 8:(j + 1) * 8],
                                               tmp_all[:, j * 16:(j + 1) * 16]))
                         for j in range(TCH)]))
        rt(lambda: vops([(lambda e, j=j: e.tensor_copy(mv2_all[:, j * 8:j * 8 + 4],
                                                       m8_all[:, j * 8:j * 8 + 4]))
                         for j in range(TCH)]))
        rt(lambda: vops([(lambda e, j=j: e.match_replace(tmp2_all[:, j * 16:(j + 1) * 16],
                                                         mv2_all[:, j * 8:(j + 1) * 8],
                                                         tmp_all[:, j * 16:(j + 1) * 16], NEG))
                         for j in range(TCH)]))
        rt(lambda: vops([lambda e: e.tensor_single_scalar(sel_all[:, :], tmp2_all[:, :],
                                                          NEG, OP.is_equal)]))
        rt(lambda: vops([lambda e: e.tensor_mul(c_all[:, :], s_all[:, :], sel_all[:, :])]))
        rt(lambda: vops([lambda e: e.reduce_sum(
            den_r[:, 0:16], c_all[:, :].rearrange("p (j e) -> p j e", e=NE), axis=AX)]))
        rt(lambda: vops([lambda e: e.tensor_scalar_add(den_r[:, 16:32], den_r[:, 0:16], 1e-20)]))
        rt(lambda: vops([lambda e: e.reciprocal(den_r[:, 32:48], den_r[:, 16:32])]))

        def _rt_final():
            v = vops([(lambda e, j=j: e.tensor_scalar_mul(
                c_all[:, j * NE:(j + 1) * NE], c_all[:, j * NE:(j + 1) * NE],
                den_r[:, 32 + j:33 + j])) for j in range(TCH)])
            for j in range(TCH):
                v_rout_c[j] = v
        rt(_rt_final)

        if taps:
            w("sync", dv, v_transp_all)
            dma("sync", tp, dbg_h, y_acc[:, :])
            dma("sync", tp, dbg_c, c_all[:, :])
            dma("sync", tp, dbg_s, s_all[:, :])

        emit_gu(1)
        while routing_thunks:
            routing_thunks.pop(0)()
        for e_i in range(NE - 2):
            emit_down(e_i)
            if e_i + 2 < NE:
                emit_gu(e_i + 2)
        emit_tail_merged()


        if dbg is not None:
            dbg.update(dict(A=A, B=B, y_acc=y_acc, Sb=Sb, Gb=Gb, Ew=Ew, Cw=Cw,
                            W2=W2, Dw=Dw, gw_sb=gw_sb, eb_sb=eb_sb, s_all=s_all,
                            c_all=c_all, st_sum=st_sum, st_sq=st_sq, st_nm=st_nm,
                            st_rs=st_rs, s2_nm=s2_nm, s2_rs=s2_rs))

        with nc.Block() as block:
            @block.sync
            def _(eng):
                for f in ops["sync"]:
                    f(eng)

            @block.gpsimd
            def _(eng):
                for f in ops["gpsimd"]:
                    f(eng)

            @block.tensor
            def _(eng):
                for f in ops["tensor"]:
                    f(eng)

            @block.vector
            def _(eng):
                for f in ops["vector"]:
                    f(eng)

            @block.scalar
            def _(eng):
                for f in ops["scalar"]:
                    f(eng)

    return nc


def kernel(L0_attn_x, L0_attn_w_q, L0_attn_w_k, L0_attn_w_v,
           L0_ffn_gate_weight, L0_ffn_e_bias, L0_ffn_gate_proj,
           L0_ffn_up_proj, L0_ffn_down_proj):
    x = np.asarray(L0_attn_x, dtype=np.float32)
    wq = np.asarray(L0_attn_w_q, dtype=np.float32)
    wk = np.asarray(L0_attn_w_k, dtype=np.float32)
    wv = np.asarray(L0_attn_w_v, dtype=np.float32)
    gw = np.asarray(L0_ffn_gate_weight, dtype=np.float32)
    eb = np.asarray(L0_ffn_e_bias, dtype=np.float32)
    gp = np.asarray(L0_ffn_gate_proj, dtype=np.float32)
    up = np.asarray(L0_ffn_up_proj, dtype=np.float32)
    dp = np.asarray(L0_ffn_down_proj, dtype=np.float32)

    wqT = np.ascontiguousarray(wq.T)
    wkT = np.ascontiguousarray(wk.T)
    wv_c = np.ascontiguousarray(wv)
    gwT = np.ascontiguousarray(gw.T)
    ebb = np.ascontiguousarray(np.tile(eb[None, :], (128, TCH)))
    gpT = np.ascontiguousarray(gp.transpose(0, 2, 1))
    upT = np.ascontiguousarray(up.transpose(0, 2, 1))
    dpT = np.ascontiguousarray(dp.transpose(0, 2, 1))
    ident = np.eye(128, dtype=np.float32)
    identb = np.eye(128, dtype=ml_dtypes.bfloat16)
    epsvec = np.full((128, 1), EPS, dtype=np.float32)

    nc = build_nc()
    in_maps = []
    for core in range(NCORES):
        xt_c = np.ascontiguousarray(x[core])
        xf_c = np.ascontiguousarray(x[core].T)
        in_maps.append({
            "xt": xt_c, "xf": xf_c,
            "wqT": wqT, "wkT": wkT, "wv": wv_c,
            "gwT": gwT, "ebb": ebb,
            "gpT": gpT, "upT": upT, "dpT": dpT,
            "ident": ident, "identb": identb, "epsv": epsvec,
        })
    res = run_bass_kernel_spmd(nc, in_maps, core_ids=list(range(NCORES)))
    outs = [np.asarray(r["out"], dtype=np.float32) for r in res.results]
    return np.stack(outs, axis=0)


if __name__ == "__main__":
    nc = build_nc()
    print("build OK")







# version 32 (final)
# speedup vs baseline: 1.0089x; 1.0035x over previous
import numpy as np
import ml_dtypes
import concourse.bass as bass
import concourse.mybir as mybir
from concourse.bass_utils import run_bass_kernel_spmd
from contextlib import ExitStack

F32 = mybir.dt.float32
BF16 = mybir.dt.bfloat16
AF = mybir.ActivationFunctionType
OP = mybir.AluOpType
AX = mybir.AxisListType.X

NCORES = 8
T = 2048
D = 1024
NE = 16
IDIM = 512
TCH = T // 128
DCH = D // 128
ICH = IDIM // 128
NEG = -1.0e30
GAMMA = 32.0
EPS = 1e-5


class Ctr:
    def __init__(self, sem):
        self.sem = sem
        self.n = 0

    def inc(self, k=1):
        self.n += k
        return self.n


def build_nc(dbg=None, taps=False):
    nc = bass.Bass("TRN2", target_bir_lowering=False)

    xt = nc.dram_tensor("xt", [T, D], F32, kind="ExternalInput").ap()
    xf = nc.dram_tensor("xf", [D, T], F32, kind="ExternalInput").ap()
    wqT = nc.dram_tensor("wqT", [D, D], F32, kind="ExternalInput").ap()
    wkT = nc.dram_tensor("wkT", [D, D], F32, kind="ExternalInput").ap()
    wv = nc.dram_tensor("wv", [D, D], F32, kind="ExternalInput").ap()
    gwT = nc.dram_tensor("gwT", [D, NE], F32, kind="ExternalInput").ap()
    ebb = nc.dram_tensor("ebb", [128, TCH * NE], F32, kind="ExternalInput").ap()
    gpT = nc.dram_tensor("gpT", [NE, D, IDIM], F32, kind="ExternalInput").ap()
    upT = nc.dram_tensor("upT", [NE, D, IDIM], F32, kind="ExternalInput").ap()
    dpT = nc.dram_tensor("dpT", [NE, IDIM, D], F32, kind="ExternalInput").ap()
    ident = nc.dram_tensor("ident", [128, 128], F32, kind="ExternalInput").ap()
    identb = nc.dram_tensor("identb", [128, 128], BF16, kind="ExternalInput").ap()
    epsv = nc.dram_tensor("epsv", [128, 1], F32, kind="ExternalInput").ap()
    out = nc.dram_tensor("out", [T, D], F32, kind="ExternalOutput").ap()
    if taps:
        dbg_S = nc.dram_tensor("dbg_S", [128, 8192], BF16, kind="ExternalOutput").ap()
        dbg_G3 = nc.dram_tensor("dbg_G3", [128, 8192], BF16, kind="ExternalOutput").ap()
        dbg_h = nc.dram_tensor("dbg_h", [128, 16384], F32, kind="ExternalOutput").ap()
        dbg_c = nc.dram_tensor("dbg_c", [128, 256], F32, kind="ExternalOutput").ap()
        dbg_s = nc.dram_tensor("dbg_s", [128, 256], F32, kind="ExternalOutput").ap()

    es = ExitStack()

    cur = [((nc.sbuf_base + 63) // 64) * 64]

    def sb(name, shape, dtype, at=None):
        nbytes = shape[1] * (2 if dtype == BF16 else 4)
        if at is None:
            at = cur[0]
            cur[0] = at + ((nbytes + 63) // 64) * 64
        return nc.alloc_sbuf_tensor_at(name, shape, dtype, offset=at)

    def psum(name, shape, dtype):
        return es.enter_context(nc.psum_tensor(name, shape, dtype))

    def sem(name):
        return Ctr(es.enter_context(nc.semaphore(name)))

    with es:
        A = sb("A", [128, 16384], BF16)
        B = sb("B", [128, 16384], BF16)
        y_acc = sb("y_acc", [128, 16384], F32)
        ybase = y_acc.manual_sbuf_range[0]
        Sb = sb("Sal", [128, 8192], BF16, at=ybase)
        Gb = sb("G1al", [128, 8192], BF16, at=ybase + 16384)
        Ew = sb("WVal", [128, 8192], BF16, at=ybase + 32768)
        Cw = sb("G2al", [128, 8192], BF16, at=ybase + 49152)
        W2 = sb("W2", [128, 16384], BF16)
        Dw = sb("G3b", [128, 8192], BF16)
        gw_sb = sb("gw_sb", [128, DCH * NE], BF16)
        eb_sb = sb("eb_sb", [128, TCH * NE], F32)
        id_sb = sb("id_sb", [128, 128], F32)
        idb_sb = sb("idb_sb", [128, 128], BF16)
        eps_sb = sb("eps_sb", [128, 1], F32)
        dum_sb = sb("dum_sb", [128, 1], F32)
        s_all = sb("s_all", [128, TCH * NE], F32)
        c_all = sb("c_all", [128, TCH * NE], F32)
        sq_scr = sb("sq_scr", [128, 1024], BF16)
        sg_scr = sb("sg_scr", [128, 1024], BF16)
        hb_rot = sb("hb_rot", [128, 4096], BF16)
        sc_all = sb("sc_all", [128, 256], F32)
        ps_all = sb("ps_all", [128, 384], F32)
        p8_all = sb("p8_all", [128, 128], F32)
        p8b_all = sb("p8b_all", [128, 128], F32)
        m8_all = sb("m8_all", [128, 128], F32)
        mv_all = sb("mv_all", [128, 128], F32)
        mv2_all = sb("mv2_all", [128, 128], F32)
        gm_all = sb("gm_all", [128, 64], F32)
        gmx_all = sb("gmx_all", [128, 256], F32)
        tmp_all = sb("tmp_all", [128, 256], F32)
        tmp2_all = sb("tmp2_all", [128, 256], F32)
        sel_all = sb("sel_all", [128, 256], F32)
        den_r = sb("den_r", [128, 48], F32)
        st_sum = sb("st_sum", [128, TCH], F32)
        st_ha = sb("st_ha", [128, TCH], F32)
        st_hb = sb("st_hb", [128, TCH], F32)
        st_sq = sb("st_sq", [128, TCH], F32)
        st_nm = sb("st_nm", [128, TCH], F32)
        st_var = sb("st_var", [128, TCH], F32)
        st_std = sb("st_std", [128, TCH], F32)
        st_rs = sb("st_rs", [128, TCH], F32)
        st_a = sb("st_a", [128, TCH], F32)
        s2_sum = sb("s2_sum", [128, TCH], F32)
        s2_ha = sb("s2_ha", [128, TCH], F32)
        s2_hb = sb("s2_hb", [128, TCH], F32)
        s2_sq = sb("s2_sq", [128, TCH], F32)
        s2_nm = sb("s2_nm", [128, TCH], F32)
        s2_var = sb("s2_var", [128, TCH], F32)
        s2_std = sb("s2_std", [128, TCH], F32)
        s2_rs = sb("s2_rs", [128, TCH], F32)
        s2_a = sb("s2_a", [128, TCH], F32)

        ps = [psum(f"ps{i}", [128, 512], F32) for i in range(8)]
        pr2_ = [ps[6], ps[7]]

        dXt = sem("dXt")
        dXf = sem("dXf")
        dW = sem("dW")
        dWv = sem("dWv")
        dGw = sem("dGw")
        dFgu = [sem("dFgu0"), sem("dFgu1")]
        dFdp = [sem("dFdp0"), sem("dFdp1")]
        ds = sem("ds")
        do = sem("do")
        pe = sem("pe")
        dv = sem("dv")
        ac = sem("ac")
        tp = sem("tp")

        ops = {k: [] for k in ("sync", "gpsimd", "tensor", "vector", "scalar")}

        def em(eng, f):
            ops[eng].append(f)

        def w(eng, ctr, val):
            if val > 0:
                em(eng, lambda e, s=ctr.sem, v=val: e.wait_ge(s, v))

        def vstep(build):
            vv = dv.inc()
            em("vector", lambda e, b=build, s=dv.sem: b(e).then_inc(s, 1))
            em("vector", lambda e, s=dv.sem, v=vv: e.wait_ge(s, v))
            return vv

        slot_state = [None] * 8
        slot_rr = [0]
        slot_lim = [6]

        def acquire_slot(eng="tensor"):
            i = slot_rr[0] % slot_lim[0]
            slot_rr[0] += 1
            if slot_state[i] is not None:
                c, v = slot_state[i]
                w(eng, c, v)
            return i

        p8_sent = p8_all[:, :].rearrange("p (j c) -> p j c", c=8)[:, :, 4:8]
        mv_sent = mv_all[:, :].rearrange("p (j c) -> p j c", c=8)[:, :, 2:8]
        mv2_sent = mv2_all[:, :].rearrange("p (j c) -> p j c", c=8)[:, :, 4:8]
        em("vector", lambda e: e.memset(p8_sent, -3.0e30))
        em("vector", lambda e: e.memset(mv_sent, 3.0e30))
        vstep(lambda e: e.memset(mv2_sent, 3.0e30))

        def dma(eng, ctr, outap, inap):
            v = ctr.inc(16)
            em(eng, lambda e, o=outap, i=inap, s=ctr.sem: e.dma_start(out=o, in_=i).then_inc(s, 16))
            return v

        xt_r = xt.rearrange("(j p) d -> p j d", p=128)
        A_r = A[:, :].rearrange("p (j d) -> p j d", d=D)
        dma("gpsimd", dXt, A_r, xt_r)
        wqT_r = wqT.rearrange("(c p) d -> p c d", p=128)
        dma("gpsimd", dW, W2[:, 0:8192].rearrange("p (c d) -> p c d", d=D), wqT_r)
        wkT_r = wkT.rearrange("(c p) d -> p c d", p=128)
        dma("gpsimd", dW, W2[:, 8192:16384].rearrange("p (c d) -> p c d", d=D), wkT_r)
        vdWqk = dW.n
        xf_r = xf.rearrange("(c p) t -> p c t", p=128)
        B_r = B[:, :].rearrange("p (c t) -> p c t", t=T)
        dma("gpsimd", dXf, B_r, xf_r)

        gwT_r = gwT.rearrange("(c p) e -> p c e", p=128)
        gw_r = gw_sb[:, :].rearrange("p (c e) -> p c e", e=NE)
        dma("gpsimd", dGw, gw_r, gwT_r)
        wv_r = wv.rearrange("(c p) d -> p c d", p=128)
        Ew_r = Ew[:, :].rearrange("p (c d) -> p c d", d=D)
        dma("gpsimd", dWv, Ew_r, wv_r)
        vdWv = dWv.n

        dma("sync", ds, id_sb[:, :], ident)
        dma("sync", ds, idb_sb[:, :], identb)
        w("scalar", ds, 64)
        vac0 = ac.inc()
        em("scalar", lambda e, s=ac.sem:
           e.activation(dum_sb[:, 0:1], eps_sb[:, 0:1], AF.Sqrt).then_inc(s, 1))
        dma("sync", ds, eb_sb[:, :], ebb)
        dma("sync", ds, eps_sb[:, :], epsv)
        vds = ds.n

        w("tensor", ds, vds)
        for k_ in range(200):
            si = acquire_slot()
            if k_ == 199:
                vpe = pe.inc()
                em("tensor", lambda e, o=ps[si][:, 0:128], s=pe.sem:
                   e.matmul(o, idb_sb[:, :], idb_sb[:, :], start=True, stop=True).then_inc(s, 1))
                for i_ in range(6):
                    slot_state[i_] = (pe, vpe)
            else:
                em("tensor", lambda e, o=ps[si][:, 0:128]:
                   e.matmul(o, idb_sb[:, :], idb_sb[:, :], start=True, stop=True))

        w("tensor", dXt, 16)
        v_s_tile = {}
        for d1c in range(DCH):
            for d2h in range(2):
                if d2h == 0 and d1c >= 4:
                    continue
                si = acquire_slot()
                pst = ps[si]
                for j in range(TCH):
                    last = j == TCH - 1
                    inst_args = (pst[:, :],
                                 A[:, j * D + d1c * 128: j * D + (d1c + 1) * 128],
                                 A[:, j * D + d2h * 512: j * D + d2h * 512 + 512],
                                 j == 0, last)
                    if last:
                        vpe = pe.inc()
                        em("tensor", lambda e, a=inst_args, s=pe.sem:
                           e.matmul(a[0], a[1], a[2], start=a[3], stop=a[4]).then_inc(s, 1))
                    else:
                        em("tensor", lambda e, a=inst_args:
                           e.matmul(a[0], a[1], a[2], start=a[3], stop=a[4]))
                w("vector", pe, vpe)
                vdv = dv.inc()
                dst = Sb[:, d1c * D + d2h * 512: d1c * D + d2h * 512 + 512]
                em("vector", lambda e, o=dst, i=pst[:, :], s=dv.sem:
                   e.tensor_copy(o, i).then_inc(s, 1))
                slot_state[si] = (dv, vdv)
                v_s_tile[(d1c, d2h)] = vdv

        v_s_direct = dv.n
        w("tensor", dW, vdWqk)
        for bc in range(DCH):
            for ah in range(2):
                si = acquire_slot()
                pst = ps[si]
                for kc in range(DCH):
                    last = kc == DCH - 1
                    a = (pst[:, :],
                         W2[:, 8192 + kc * D + bc * 128: 8192 + kc * D + (bc + 1) * 128],
                         W2[:, kc * D + ah * 512: kc * D + ah * 512 + 512],
                         kc == 0, last)
                    if last:
                        vpe = pe.inc()
                        em("tensor", lambda e, a=a, s=pe.sem:
                           e.matmul(a[0], a[1], a[2], start=a[3], stop=a[4]).then_inc(s, 1))
                    else:
                        em("tensor", lambda e, a=a:
                           e.matmul(a[0], a[1], a[2], start=a[3], stop=a[4]))
                w("vector", pe, vpe)
                vdv = dv.inc()
                dst = Gb[:, bc * D + ah * 512: bc * D + ah * 512 + 512]
                em("vector", lambda e, o=dst, i=pst[:, :], s=dv.sem:
                   e.tensor_copy(o, i).then_inc(s, 1))
                slot_state[si] = (dv, vdv)
        v_g1_mm_done = pe.n
        v_g1_evac = dv.n

        w("tensor", ds, vds)
        w("tensor", dv, v_s_tile[(3, 1)])
        for d1c in range(4, DCH):
            si = acquire_slot()
            pst = ps[si]
            for d2c in range(4):
                last = d2c == 3
                srcb = Sb[:, d2c * D + d1c * 128: d2c * D + (d1c + 1) * 128]
                dstp = pst[:, d2c * 128: (d2c + 1) * 128]
                if last:
                    vpe = pe.inc()
                    em("tensor", lambda e, o=dstp, i=srcb, s=pe.sem:
                       e.matmul(o, i, idb_sb[:, :], start=True, stop=True).then_inc(s, 1))
                else:
                    em("tensor", lambda e, o=dstp, i=srcb:
                       e.matmul(o, i, idb_sb[:, :], start=True, stop=True))
            w("vector", pe, vpe)
            vdv = dv.inc()
            dst = Sb[:, d1c * D: d1c * D + 512]
            em("vector", lambda e, o=dst, i=pst[:, :], s=dv.sem:
               e.tensor_copy(o, i).then_inc(s, 1))
            slot_state[si] = (dv, vdv)

        v_s_evac = dv.n
        if taps:
            w("gpsimd", dv, v_s_evac)
            dma("gpsimd", tp, dbg_S, Sb[:, :])

        v_g2_tile = {}
        for ah in range(2):
            for d2c in range(DCH - 1, -1, -1):
                w("tensor", dv, v_g1_evac if d2c >= 4 else v_s_evac)
                si = acquire_slot()
                pst = ps[si]
                for bc in range(DCH):
                    last = bc == DCH - 1
                    a = (pst[:, :],
                         Sb[:, bc * D + d2c * 128: bc * D + (d2c + 1) * 128],
                         Gb[:, bc * D + ah * 512: bc * D + ah * 512 + 512],
                         bc == 0, last)
                    if last:
                        vpe = pe.inc()
                        em("tensor", lambda e, a=a, s=pe.sem:
                           e.matmul(a[0], a[1], a[2], start=a[3], stop=a[4]).then_inc(s, 1))
                    else:
                        em("tensor", lambda e, a=a:
                           e.matmul(a[0], a[1], a[2], start=a[3], stop=a[4]))
                w("vector", pe, vpe)
                vdv = dv.inc()
                dst = Cw[:, d2c * D + ah * 512: d2c * D + ah * 512 + 512]
                em("vector", lambda e, o=dst, i=pst[:, :], s=dv.sem:
                   e.tensor_copy(o, i).then_inc(s, 1))
                slot_state[si] = (dv, vdv)
                v_g2_tile[(d2c, ah)] = vdv
        v_g2_mm_done = pe.n
        v_g2_evac = dv.n

        wset = [
            (W2[:, 0:4096], W2[:, 4096:8192], W2[:, 8192:12288]),
            (W2[:, 12288:16384], Dw[:, 0:4096], Dw[:, 4096:8192]),
        ]

        v_dFgu = [0, 0]
        v_dFdp = [0, 0]

        def wdma_gu(e_idx):
            p = e_idx % 2
            gdst, udst, _ = wset[p]
            dma("gpsimd", dFgu[p], gdst.rearrange("p (c i) -> p c i", i=IDIM),
                gpT[e_idx].rearrange("(c p) i -> p c i", p=128))
            dma("gpsimd", dFgu[p], udst.rearrange("p (c i) -> p c i", i=IDIM),
                upT[e_idx].rearrange("(c p) i -> p c i", p=128))
            v_dFgu[p] = dFgu[p].n

        def wdma_dp(e_idx):
            p = e_idx % 2
            dma("gpsimd", dFdp[p], wset[p][2].rearrange("p (c d) -> p c d", d=D),
                dpT[e_idx].rearrange("(c p) d -> p c d", p=128))
            v_dFdp[p] = dFdp[p].n

        w("gpsimd", pe, v_g1_mm_done)
        wdma_gu(0)
        wdma_dp(0)
        dma("gpsimd", dFgu[1], wset[1][0].rearrange("p (c i) -> p c i", i=IDIM),
            gpT[1].rearrange("(c p) i -> p c i", p=128))

        w("tensor", dWv, vdWv)
        v_g3_tile = {}
        for dh in range(2):
            for acc_ in range(DCH):
                w("tensor", dv, max(v_g2_tile[(d2c_, acc_ // 4)] for d2c_ in range(DCH)))
                si = acquire_slot()
                pst = ps[si]
                for d2c in range(DCH):
                    last = d2c == DCH - 1
                    a = (pst[:, :],
                         Cw[:, d2c * D + acc_ * 128: d2c * D + (acc_ + 1) * 128],
                         Ew[:, d2c * D + dh * 512: d2c * D + dh * 512 + 512],
                         d2c == 0, last)
                    if last:
                        vpe = pe.inc()
                        em("tensor", lambda e, a=a, s=pe.sem:
                           e.matmul(a[0], a[1], a[2], start=a[3], stop=a[4]).then_inc(s, 1))
                    else:
                        em("tensor", lambda e, a=a:
                           e.matmul(a[0], a[1], a[2], start=a[3], stop=a[4]))
                w("scalar", pe, vpe)
                vac = ac.inc()
                dst = Dw[:, acc_ * D + dh * 512: acc_ * D + dh * 512 + 512]
                em("scalar", lambda e, o=dst, i=pst[:, :], s=ac.sem:
                   e.activation(o, i, AF.Copy, scale=GAMMA).then_inc(s, 1))
                slot_state[si] = (ac, vac)
                v_g3_tile[(acc_, dh)] = vac
        v_g3_mm_done = pe.n
        v_g3_evac = ac.n

        if taps:
            w("sync", ac, v_g3_evac)
            dma("sync", tp, dbg_G3, Dw[:, :])

        v_tp = {}
        v_tp_pe = {}
        v_sig_bank = [0, 0]

        def norm_cast(j):
            yj = y_acc[:, j * D: (j + 1) * D]
            hbj = hb_rot[:, (j % 4) * D: (j % 4 + 1) * D]
            if j % 3 != 1:
                w("vector", dv, v_ln1q[j // 4])
                if j >= 4:
                    w("vector", pe, v_tp_pe[(j - 4, 1)])
                vdv = dv.inc()
                em("vector", lambda e, o=hbj, i=yj, m=st_nm[:, j:j + 1],
                   r=st_rs[:, j:j + 1], s=dv.sem:
                   e.tensor_scalar(o, i, m, r, OP.add, OP.mult).then_inc(s, 1))
                v_norm[j] = (dv, vdv)
            else:
                w("scalar", dv, v_ln1q[j // 4])
                if j >= 4:
                    w("scalar", pe, v_tp_pe[(j - 4, 1)])
                vac = ac.inc()
                em("scalar", lambda e, o=hbj, i=yj, r=st_rs[:, j:j + 1],
                   b=st_a[:, j:j + 1], s=ac.sem:
                   e.activation(o, i, AF.Identity, scale=r, bias=b).then_inc(s, 1))
                v_norm[j] = (ac, vac)

        def norm_fp32(j):
            yj = y_acc[:, j * D: (j + 1) * D]
            if j % 3 != 1:
                c_, v_ = v_norm[j]
                w("vector", c_, v_)
                w("vector", ac, v_sq_q1[j // 4])
                vdv = dv.inc()
                em("vector", lambda e, i=yj, m=st_nm[:, j:j + 1], r=st_rs[:, j:j + 1],
                   s=dv.sem:
                   e.tensor_scalar(i, i, m, r, OP.add, OP.mult).then_inc(s, 1))
                v_hf[j] = (dv, vdv)
            else:
                c_, v_ = v_norm[j]
                w("scalar", c_, v_)
                w("scalar", ac, v_sq_q1[j // 4])
                vac = ac.inc()
                em("scalar", lambda e, i=yj, r=st_rs[:, j:j + 1], b=st_a[:, j:j + 1],
                   s=ac.sem:
                   e.activation(i, i, AF.Identity, scale=r, bias=b).then_inc(s, 1))
                v_hf[j] = (ac, vac)

        def packs(j):
            w("tensor", v_norm[j][0], v_norm[j][1])
            for g in range(2):
                if j == 0:
                    si = 6 + g
                else:
                    si = acquire_slot()
                pst = ps[si]
                for q in range(4):
                    c = g * 4 + q
                    last = q == 3
                    srcp = hb_rot[:, (j % 4) * D + c * 128: (j % 4) * D + (c + 1) * 128]
                    dstp = pst[:, q * 128: (q + 1) * 128]
                    if last:
                        vpe = pe.inc()
                        em("tensor", lambda e, o=dstp, i=srcp, s=pe.sem:
                           e.matmul(o, i, idb_sb[:, :], start=True, stop=True).then_inc(s, 1))
                    else:
                        em("tensor", lambda e, o=dstp, i=srcp:
                           e.matmul(o, i, idb_sb[:, :], start=True, stop=True))
                v_tp_pe[(j, g)] = vpe
                dst = A[:, :].rearrange("p (c t) -> p c t", t=T)[:, g * 4:(g + 1) * 4,
                                                                j * 128:(j + 1) * 128]
                srcv = pst[:, :].rearrange("p (c t) -> p c t", t=128)
                if (2 * j + g) % 2 == 0:
                    w("vector", pe, vpe)
                    w("vector", dv, v_r1[TCH - 1])
                    vv = dv.inc()
                    em("vector", lambda e, o=dst, i=srcv, s=dv.sem:
                       e.tensor_copy(o, i).then_inc(s, 1))
                    slot_state[si] = (dv, vv)
                    v_tp[(j, g)] = (dv, vv)
                else:
                    w("scalar", pe, vpe)
                    w("scalar", dv, v_r1[TCH - 1])
                    vv = ac.inc()
                    em("scalar", lambda e, o=dst, i=srcv, s=ac.sem:
                       e.activation(o, i, AF.Copy).then_inc(s, 1))
                    slot_state[si] = (ac, vv)
                    v_tp[(j, g)] = (ac, vv)

        def routing_mm(j):
            for g in range(2):
                c_, v_ = v_tp[(j, g)]
                w("tensor", c_, v_)
            prj = pr2_[j % 2]
            if v_sig_bank[j % 2]:
                w("tensor", ac, v_sig_bank[j % 2])
            for kc in range(DCH):
                last = kc == DCH - 1
                a = (prj[:, 0:NE],
                     A[:, kc * T + j * 128: kc * T + (j + 1) * 128],
                     gw_sb[:, kc * NE: (kc + 1) * NE],
                     kc == 0, last)
                if last:
                    vpe = pe.inc()
                    em("tensor", lambda e, a=a, s=pe.sem:
                       e.matmul(a[0], a[1], a[2], start=a[3], stop=a[4]).then_inc(s, 1))
                else:
                    em("tensor", lambda e, a=a:
                       e.matmul(a[0], a[1], a[2], start=a[3], stop=a[4]))
            w("scalar", pe, vpe)
            vac = ac.inc()
            em("scalar", lambda e, o=s_all[:, j * NE: (j + 1) * NE], i=prj[:, 0:NE], s=ac.sem:
               e.activation(o, i, AF.Sigmoid).then_inc(s, 1))
            v_sig_j[j] = vac
            v_sig_bank[j % 2] = vac
            slot_state[6 + j % 2] = (ac, vac)
            if j == TCH - 1:
                slot_lim[0] = 8

        w("tensor", dXf, 16)
        v_r1 = [0] * TCH
        v_ln1q = [0] * 4
        v_sq_q1 = [0] * 4
        v_norm = [None] * TCH
        v_hf = [None] * TCH
        v_rout_c = [0] * TCH
        v_sig_j = [0] * TCH
        if taps:
            w("vector", tp, 32)
        for j in range(TCH):
            for dh in range(2):
                w("tensor", ac, max(v_g3_tile[(a_, dh)] for a_ in range(DCH)))
                si = acquire_slot()
                pst = ps[si]
                for acc_ in range(DCH):
                    last = acc_ == DCH - 1
                    a = (pst[:, :],
                         B[:, acc_ * T + j * 128: acc_ * T + (j + 1) * 128],
                         Dw[:, acc_ * D + dh * 512: acc_ * D + dh * 512 + 512],
                         acc_ == 0, last)
                    if last:
                        vpe = pe.inc()
                        em("tensor", lambda e, a=a, s=pe.sem:
                           e.matmul(a[0], a[1], a[2], start=a[3], stop=a[4]).then_inc(s, 1))
                    else:
                        em("tensor", lambda e, a=a:
                           e.matmul(a[0], a[1], a[2], start=a[3], stop=a[4]))
                w("vector", pe, vpe)
                vdv = dv.inc()
                yd = y_acc[:, j * D + dh * 512: j * D + dh * 512 + 512]
                xbs = A[:, j * D + dh * 512: j * D + dh * 512 + 512]
                acc = (st_ha if dh == 0 else st_hb)[:, j:j + 1]
                em("vector", lambda e, o=yd, i0=pst[:, :], i1=xbs, a=acc, s=dv.sem:
                   e.scalar_tensor_tensor(o, i0, 1.0, i1, OP.mult, OP.add,
                                          accum_out=a).then_inc(s, 1))
                slot_state[si] = (dv, vdv)
            v_r1[j] = dv.n
            w("scalar", dv, v_r1[j])
            w("scalar", ac, v_sq_prev if j > 0 else vac0)
            vac = ac.inc()
            v_sq_prev = vac
            em("scalar", lambda e, i=y_acc[:, j * D: (j + 1) * D], a=st_sq[:, j:j + 1], s=ac.sem:
               e.activation(sq_scr[:, :], i, AF.Square, accum_out=a).then_inc(s, 1))
            if j % 4 == 3:
                q = j // 4
                sl = slice(4 * q, 4 * q + 4)
                v_sq_q1[q] = vac
                w("vector", ac, vac)
                vstep(lambda e, sl=sl: e.tensor_add(st_sum[:, sl], st_ha[:, sl], st_hb[:, sl]))
                vstep(lambda e, sl=sl: e.tensor_scalar_mul(st_nm[:, sl], st_sum[:, sl], -1.0 / D))
                vstep(lambda e, sl=sl: e.tensor_scalar_mul(st_var[:, sl], st_sq[:, sl], 1.0 / D))
                vstep(lambda e, sl=sl: e.tensor_mul(st_std[:, sl], st_nm[:, sl], st_nm[:, sl]))
                vdv = vstep(lambda e, sl=sl: e.tensor_sub(st_var[:, sl], st_var[:, sl], st_std[:, sl]))
                if q == 0:
                    w("scalar", ds, vds)
                w("scalar", dv, vdv)
                vac = ac.inc()
                em("scalar", lambda e, sl=sl, s=ac.sem:
                   e.activation(st_std[:, sl], st_var[:, sl], AF.Sqrt,
                                bias=eps_sb[:, 0:1]).then_inc(s, 1))
                w("vector", ac, vac)
                vstep(lambda e, sl=sl: e.reciprocal(st_rs[:, sl], st_std[:, sl]))
                v_ln1q[q] = vstep(lambda e, sl=sl: e.tensor_mul(st_a[:, sl], st_nm[:, sl], st_rs[:, sl]))
                if q == 0:
                    for jj in range(4):
                        norm_cast(jj)
        v_y_mm_done = pe.n

        w("gpsimd", pe, v_y_mm_done)
        if taps:
            w("gpsimd", tp, 32)
        dma("gpsimd", dFgu[1], wset[1][1].rearrange("p (c i) -> p c i", i=IDIM),
            upT[1].rearrange("(c p) i -> p c i", p=128))
        v_dFgu[1] = dFgu[1].n
        dma("gpsimd", dFdp[1], wset[1][2].rearrange("p (c d) -> p c d", d=D),
            dpT[1].rearrange("(c p) d -> p c d", p=128))
        v_dFdp[1] = dFdp[1].n

        v_stt = {}
        sg_state = [None, None]
        gu_gidx = [0]
        v_act = [dict() for _ in range(NE)]
        v_gu_last = [0] * NE
        v_down_last = [0] * NE

        def emit_gu_q(e_i, tt, cb=None):
            p = e_i % 2
            pbase = p * 8192
            gp_sb, up_sb, _ = wset[p]
            if tt == 0:
                w("tensor", dFgu[p], v_dFgu[p])
            if e_i == 0:
                mx = {}
                for j_ in range(4 * tt, 4 * tt + 4):
                    for g_ in range(2):
                        c_, v_ = v_tp[(j_, g_)]
                        mx[id(c_)] = (c_, max(v_, mx.get(id(c_), (c_, 0))[1]))
                for c_, v_ in mx.values():
                    w("tensor", c_, v_)
            for ic in range(ICH):
                siG = acquire_slot()
                pstG = ps[siG]
                for hc in range(DCH):
                    last = hc == DCH - 1
                    a = (pstG[:, :],
                         gp_sb.rearrange("p (c i) -> p c i", i=IDIM)[:, hc,
                                                                    ic * 128:(ic + 1) * 128],
                         A[:, hc * T + tt * 512: hc * T + tt * 512 + 512],
                         hc == 0, last)
                    if last:
                        vpeG = pe.inc()
                        em("tensor", lambda e, a=a, s=pe.sem:
                           e.matmul(a[0], a[1], a[2], start=a[3], stop=a[4]).then_inc(s, 1))
                    else:
                        em("tensor", lambda e, a=a:
                           e.matmul(a[0], a[1], a[2], start=a[3], stop=a[4]))
                siU = acquire_slot()
                pstU = ps[siU]
                for hc in range(DCH):
                    last = hc == DCH - 1
                    a = (pstU[:, :],
                         up_sb.rearrange("p (c i) -> p c i", i=IDIM)[:, hc,
                                                                    ic * 128:(ic + 1) * 128],
                         A[:, hc * T + tt * 512: hc * T + tt * 512 + 512],
                         hc == 0, last)
                    if last:
                        vpeU = pe.inc()
                        em("tensor", lambda e, a=a, s=pe.sem:
                           e.matmul(a[0], a[1], a[2], start=a[3], stop=a[4]).then_inc(s, 1))
                    else:
                        em("tensor", lambda e, a=a:
                           e.matmul(a[0], a[1], a[2], start=a[3], stop=a[4]))
                sgslot = gu_gidx[0] % 2
                if sg_state[sgslot] is not None:
                    w("scalar", dv, sg_state[sgslot])
                w("scalar", pe, vpeG)
                vacS = ac.inc()
                sgv = sg_scr[:, sgslot * 512: sgslot * 512 + 512]
                em("scalar", lambda e, o=sgv, i=pstG[:, :], s=ac.sem:
                   e.activation(o, i, AF.Silu).then_inc(s, 1))
                slot_state[siG] = (ac, vacS)
                w("vector", pe, vpeU)
                w("vector", ac, vacS)
                if tt == 0 and ic == 0:
                    if e_i <= 1:
                        w("vector", pe, v_y_mm_done)
                    else:
                        w("vector", pe, v_down_last[e_i - 2])
                vdv = dv.inc()
                adst = B[:, pbase + ic * T + tt * 512: pbase + ic * T + tt * 512 + 512]
                em("vector", lambda e, o=adst, i0=pstU[:, :], i1=sgv, s=dv.sem:
                   e.tensor_tensor(o, i0, i1, OP.mult).then_inc(s, 1))
                slot_state[siU] = (dv, vdv)
                sg_state[sgslot] = vdv
                v_act[e_i][(tt, ic)] = vdv
                gu_gidx[0] += 1
                if e_i == 0:
                    routing_mm(4 * tt + ic)
                if e_i == 1 and routing_thunks:
                    routing_thunks.pop(0)()
                if cb is not None:
                    cb(ic)
            if tt == 3:
                v_gu_last[e_i] = pe.n
                if e_i + 2 < NE:
                    w("gpsimd", pe, v_gu_last[e_i])
                    wdma_gu(e_i + 2)

        def emit_gu(e_i):
            for tt in range(4):
                emit_gu_q(e_i, tt)

        def emit_tail_merged():
            w("scalar", ac, vac0)
            vac = ac.inc()
            em("scalar", lambda e, s=ac.sem:
               e.activation(dum_sb[:, 0:1], eps_sb[:, 0:1], AF.Sqrt).then_inc(s, 1))
            sq_prev = [vac]
            w("tensor", dFdp[0], v_dFdp[0])
            w("tensor", dFdp[1], v_dFdp[1])
            out_r = out.rearrange("(j p) d -> p j d", p=128)
            y_r = y_acc[:, :].rearrange("p (j d) -> p j d", d=D)
            v_r2 = [None] * TCH
            for j in range(TCH):
                tt = j // 4
                for e_i in (NE - 2, NE - 1):
                    p = e_i % 2
                    pbase = p * 8192
                    dp_sb = wset[p][2]
                    for dh in range(2):
                        si = acquire_slot()
                        pst = ps[si]
                        w("tensor", dv, v_act[e_i][(tt, ICH - 1)])
                        for ic in range(ICH):
                            last = ic == ICH - 1
                            a = (pst[:, :],
                                 B[:, pbase + ic * T + j * 128: pbase + ic * T + (j + 1) * 128],
                                 dp_sb.rearrange("p (c d) -> p c d", d=D)[:, ic,
                                                                         dh * 512: dh * 512 + 512],
                                 ic == 0, last)
                            if last:
                                vpe = pe.inc()
                                em("tensor", lambda e, a=a, s=pe.sem:
                                   e.matmul(a[0], a[1], a[2], start=a[3], stop=a[4]).then_inc(s, 1))
                            else:
                                em("tensor", lambda e, a=a:
                                   e.matmul(a[0], a[1], a[2], start=a[3], stop=a[4]))
                        w("vector", pe, vpe)
                        w("vector", dv, v_stt[(j, dh)])
                        vdv = dv.inc()
                        yd = y_acc[:, j * D + dh * 512: j * D + dh * 512 + 512]
                        cw = c_all[:, j * NE + e_i: j * NE + e_i + 1]
                        if e_i == NE - 1:
                            acc = (s2_ha if dh == 0 else s2_hb)[:, j:j + 1]
                            em("vector", lambda e, o=yd, i0=pst[:, :], c=cw, a=acc, s=dv.sem:
                               e.scalar_tensor_tensor(o, i0, c, o, OP.mult, OP.add,
                                                      accum_out=a).then_inc(s, 1))
                        else:
                            em("vector", lambda e, o=yd, i0=pst[:, :], c=cw, s=dv.sem:
                               e.scalar_tensor_tensor(o, i0, c, o, OP.mult, OP.add).then_inc(s, 1))
                        slot_state[si] = (dv, vdv)
                        v_stt[(j, dh)] = vdv
                w("scalar", dv, v_stt[(j, 1)])
                w("scalar", ac, sq_prev[0])
                vac = ac.inc()
                sq_prev[0] = vac
                em("scalar", lambda e, i=y_acc[:, j * D: (j + 1) * D], a=s2_sq[:, j:j + 1],
                   s=ac.sem:
                   e.activation(sq_scr[:, :], i, AF.Square, accum_out=a).then_inc(s, 1))
                if j % 4 != 3:
                    continue
                q = j // 4
                sl = slice(4 * q, 4 * q + 4)
                v_sq_q = vac
                w("vector", dv, v_stt[(j, 1)])
                vdv = vstep(lambda e, sl=sl: e.tensor_add(s2_sum[:, sl], s2_ha[:, sl],
                                                          s2_hb[:, sl]))
                w("scalar", ac, v_sq_q)
                w("scalar", dv, vdv)
                vac = ac.inc()
                em("scalar", lambda e, sl=sl, s=ac.sem:
                   e.activation(s2_nm[:, sl], s2_sum[:, sl], AF.Identity,
                                scale=-1.0 / D).then_inc(s, 1))
                vac = ac.inc()
                em("scalar", lambda e, sl=sl, s=ac.sem:
                   e.activation(s2_var[:, sl], s2_sq[:, sl], AF.Identity,
                                scale=1.0 / D).then_inc(s, 1))
                w("scalar", ac, vac)
                vac = ac.inc()
                em("scalar", lambda e, sl=sl, s=ac.sem:
                   e.activation(s2_std[:, sl], s2_nm[:, sl], AF.Square).then_inc(s, 1))
                w("vector", ac, vac)
                vdv = vstep(lambda e, sl=sl: e.tensor_sub(s2_var[:, sl], s2_var[:, sl],
                                                          s2_std[:, sl]))
                w("scalar", dv, vdv)
                vac = ac.inc()
                em("scalar", lambda e, sl=sl, s=ac.sem:
                   e.activation(s2_std[:, sl], s2_var[:, sl], AF.Sqrt,
                                bias=eps_sb[:, 0:1]).then_inc(s, 1))
                w("vector", ac, vac)
                vstep(lambda e, sl=sl: e.reciprocal(s2_rs[:, sl], s2_std[:, sl]))
                v_ab2 = vstep(lambda e, sl=sl: e.tensor_mul(s2_a[:, sl], s2_nm[:, sl],
                                                            s2_rs[:, sl]))
                w("scalar", dv, v_ab2)
                w("scalar", ac, v_sq_q)
                for k in range(4 * q, 4 * q + 4):
                    yk = y_acc[:, k * D: (k + 1) * D]
                    if k % 4 == 0:
                        vdv = dv.inc()
                        em("vector", lambda e, i=yk, m=s2_nm[:, k:k + 1],
                           r=s2_rs[:, k:k + 1], s=dv.sem:
                           e.tensor_scalar(i, i, m, r, OP.add, OP.mult).then_inc(s, 1))
                        v_r2[k] = (dv, vdv)
                    else:
                        vac = ac.inc()
                        em("scalar", lambda e, i=yk, r=s2_rs[:, k:k + 1],
                           b=s2_a[:, k:k + 1], s=ac.sem:
                           e.activation(i, i, AF.Identity, scale=r, bias=b).then_inc(s, 1))
                        v_r2[k] = (ac, vac)
                eng = "sync" if q % 2 == 0 else "scalar"
                for k in range(4 * q, 4 * q + 4):
                    w(eng, v_r2[k][0], v_r2[k][1])
                dma(eng, do, out_r[:, 4 * q:4 * q + 4, :], y_r[:, 4 * q:4 * q + 4, :])
            w("sync", do, do.n)


        def emit_down(e_i):
            p = e_i % 2
            pbase = p * 8192
            dp_sb = wset[p][2]
            w("tensor", dFdp[p], v_dFdp[p])
            for j in range(TCH):
                tt = j // 4
                for dh in range(2):
                    si = acquire_slot()
                    pst = ps[si]
                    w("tensor", dv, v_act[e_i][(tt, ICH - 1)])
                    for ic in range(ICH):
                        last = ic == ICH - 1
                        a = (pst[:, :],
                             B[:, pbase + ic * T + j * 128: pbase + ic * T + (j + 1) * 128],
                             dp_sb.rearrange("p (c d) -> p c d", d=D)[:, ic,
                                                                     dh * 512: dh * 512 + 512],
                             ic == 0, last)
                        if last:
                            vpe = pe.inc()
                            em("tensor", lambda e, a=a, s=pe.sem:
                               e.matmul(a[0], a[1], a[2], start=a[3], stop=a[4]).then_inc(s, 1))
                        else:
                            em("tensor", lambda e, a=a:
                               e.matmul(a[0], a[1], a[2], start=a[3], stop=a[4]))
                    w("vector", pe, vpe)
                    if e_i == 0:
                        w("vector", dv, v_rout_c[j])
                        w("vector", v_hf[j][0], v_hf[j][1])
                        if j == 0 and dh == 0:
                            w("vector", dv, v_transp_all)
                            w("vector", ac, v_transp_ac)
                            if taps:
                                w("vector", tp, 80)
                    else:
                        w("vector", dv, v_stt[(j, dh)])
                    vdv = dv.inc()
                    yd = y_acc[:, j * D + dh * 512: j * D + dh * 512 + 512]
                    cw = c_all[:, j * NE + e_i: j * NE + e_i + 1]
                    if e_i == NE - 1:
                        acc = (s2_ha if dh == 0 else s2_hb)[:, j:j + 1]
                        em("vector", lambda e, o=yd, i0=pst[:, :], c=cw, a=acc, s=dv.sem:
                           e.scalar_tensor_tensor(o, i0, c, o, OP.mult, OP.add,
                                                  accum_out=a).then_inc(s, 1))
                    else:
                        em("vector", lambda e, o=yd, i0=pst[:, :], c=cw, s=dv.sem:
                           e.scalar_tensor_tensor(o, i0, c, o, OP.mult, OP.add).then_inc(s, 1))
                    slot_state[si] = (dv, vdv)
                    v_stt[(j, dh)] = vdv
            v_down_last[e_i] = pe.n
            if e_i + 2 < NE:
                w("gpsimd", pe, v_down_last[e_i])
                wdma_dp(e_i + 2)

        w("sync", ds, vds)
        w("tensor", ds, vds)
        w("tensor", dGw, 16)
        for j in range(4):
            packs(j)
            norm_cast(j + 4)

        def _mk_cb(tt):
            if tt >= 3:
                return None
            nxt = [4 * (tt + 1) + i_ for i_ in range(4)]

            def cb(ic):
                for j in ([nxt[0], nxt[1]] if ic == 0 else
                          [nxt[2]] if ic == 1 else
                          [nxt[3]] if ic == 2 else []):
                    packs(j)
                    if j + 4 < TCH:
                        norm_cast(j + 4)
            return cb

        for tt in range(4):
            emit_gu_q(0, tt, cb=_mk_cb(tt))
            for j in range(4 * tt, 4 * tt + 4):
                norm_fp32(j)

        v_transp_all = dv.n
        v_transp_ac = ac.n

        vfence = [dv.n]

        def vops(builders):
            w("vector", dv, vfence[0])
            for b in builders[:-1]:
                em("vector", b)
            vv = dv.inc()
            em("vector", lambda e, b=builders[-1], s=dv.sem: b(e).then_inc(s, 1))
            vfence[0] = vv
            return vv

        routing_thunks = []
        rt = routing_thunks.append
        rt(lambda: (w("vector", ac, v_sig_j[TCH - 1]), w("vector", ds, vds),
                    vops([lambda e: e.tensor_add(sc_all[:, :], s_all[:, :], eb_sb[:, :])])))
        sc_v = sc_all[:, :].rearrange("p (j g e) -> p e j g", e=4, g=4)
        ps_v = ps_all[:, :].rearrange("p (j q g) -> p q j g", q=6, g=4)
        pairs = [(0, 1), (0, 2), (0, 3), (1, 2), (1, 3), (2, 3)]
        rt(lambda: vops([(lambda e, q=q, i0_=i0_, i1_=i1_:
                          e.tensor_add(ps_v[:, q], sc_v[:, i0_], sc_v[:, i1_]))
                         for q, (i0_, i1_) in enumerate(pairs)]))
        rt(lambda: vops([lambda e: e.reduce_max(
            p8_all[:, :].rearrange("p (j c) -> p j c", c=8)[:, :, 0:4],
            ps_all[:, :].rearrange("p (j q g) -> p j g q", q=6, g=4), axis=AX)]))
        rt(lambda: vops([(lambda e, j=j: e.max(m8_all[:, j * 8:(j + 1) * 8],
                                               p8_all[:, j * 8:(j + 1) * 8]))
                         for j in range(TCH)]))
        rt(lambda: vops([(lambda e, j=j: e.tensor_copy(mv_all[:, j * 8:j * 8 + 2],
                                                       m8_all[:, j * 8:j * 8 + 2]))
                         for j in range(TCH)]))
        rt(lambda: vops([(lambda e, j=j: e.match_replace(p8b_all[:, j * 8:(j + 1) * 8],
                                                         mv_all[:, j * 8:(j + 1) * 8],
                                                         p8_all[:, j * 8:(j + 1) * 8], NEG))
                         for j in range(TCH)]))
        rt(lambda: vops([lambda e: e.tensor_single_scalar(
            gm_all[:, :].rearrange("p (j g) -> p j g", g=4),
            p8b_all[:, :].rearrange("p (j c) -> p j c", c=8)[:, :, 0:4],
            NEG, OP.is_equal)]))
        gmv = gm_all[:, :].rearrange("p (j g) -> p j g", g=4)
        gxv = gmx_all[:, :].rearrange("p (j g e) -> p e j g", g=4, e=4)
        rt(lambda: vops([(lambda e, k=k: e.tensor_copy(gxv[:, k], gmv)) for k in range(4)]))
        rt(lambda: vops([lambda e: e.tensor_mul(tmp_all[:, :], sc_all[:, :], gmx_all[:, :])]))
        rt(lambda: vops([(lambda e, j=j: e.max(m8_all[:, j * 8:(j + 1) * 8],
                                               tmp_all[:, j * 16:(j + 1) * 16]))
                         for j in range(TCH)]))
        rt(lambda: vops([(lambda e, j=j: e.tensor_copy(mv2_all[:, j * 8:j * 8 + 4],
                                                       m8_all[:, j * 8:j * 8 + 4]))
                         for j in range(TCH)]))
        rt(lambda: vops([(lambda e, j=j: e.match_replace(tmp2_all[:, j * 16:(j + 1) * 16],
                                                         mv2_all[:, j * 8:(j + 1) * 8],
                                                         tmp_all[:, j * 16:(j + 1) * 16], NEG))
                         for j in range(TCH)]))
        rt(lambda: vops([lambda e: e.tensor_single_scalar(sel_all[:, :], tmp2_all[:, :],
                                                          NEG, OP.is_equal)]))
        rt(lambda: vops([lambda e: e.tensor_mul(c_all[:, :], s_all[:, :], sel_all[:, :])]))
        rt(lambda: vops([lambda e: e.reduce_sum(
            den_r[:, 0:16], c_all[:, :].rearrange("p (j e) -> p j e", e=NE), axis=AX)]))
        rt(lambda: vops([lambda e: e.tensor_scalar_add(den_r[:, 16:32], den_r[:, 0:16], 1e-20)]))
        rt(lambda: vops([lambda e: e.reciprocal(den_r[:, 32:48], den_r[:, 16:32])]))

        def _rt_final():
            v = vops([(lambda e, j=j: e.tensor_scalar_mul(
                c_all[:, j * NE:(j + 1) * NE], c_all[:, j * NE:(j + 1) * NE],
                den_r[:, 32 + j:33 + j])) for j in range(TCH)])
            for j in range(TCH):
                v_rout_c[j] = v
        rt(_rt_final)

        if taps:
            w("sync", dv, v_transp_all)
            dma("sync", tp, dbg_h, y_acc[:, :])
            dma("sync", tp, dbg_c, c_all[:, :])
            dma("sync", tp, dbg_s, s_all[:, :])

        emit_gu(1)
        while routing_thunks:
            routing_thunks.pop(0)()
        for e_i in range(NE - 2):
            emit_down(e_i)
            if e_i + 2 < NE:
                emit_gu(e_i + 2)
        emit_tail_merged()


        if dbg is not None:
            dbg.update(dict(A=A, B=B, y_acc=y_acc, Sb=Sb, Gb=Gb, Ew=Ew, Cw=Cw,
                            W2=W2, Dw=Dw, gw_sb=gw_sb, eb_sb=eb_sb, s_all=s_all,
                            c_all=c_all, st_sum=st_sum, st_sq=st_sq, st_nm=st_nm,
                            st_rs=st_rs, s2_nm=s2_nm, s2_rs=s2_rs))

        with nc.Block() as block:
            @block.sync
            def _(eng):
                for f in ops["sync"]:
                    f(eng)

            @block.gpsimd
            def _(eng):
                for f in ops["gpsimd"]:
                    f(eng)

            @block.tensor
            def _(eng):
                for f in ops["tensor"]:
                    f(eng)

            @block.vector
            def _(eng):
                for f in ops["vector"]:
                    f(eng)

            @block.scalar
            def _(eng):
                for f in ops["scalar"]:
                    f(eng)

    return nc


def kernel(L0_attn_x, L0_attn_w_q, L0_attn_w_k, L0_attn_w_v,
           L0_ffn_gate_weight, L0_ffn_e_bias, L0_ffn_gate_proj,
           L0_ffn_up_proj, L0_ffn_down_proj):
    x = np.asarray(L0_attn_x, dtype=np.float32)
    wq = np.asarray(L0_attn_w_q, dtype=np.float32)
    wk = np.asarray(L0_attn_w_k, dtype=np.float32)
    wv = np.asarray(L0_attn_w_v, dtype=np.float32)
    gw = np.asarray(L0_ffn_gate_weight, dtype=np.float32)
    eb = np.asarray(L0_ffn_e_bias, dtype=np.float32)
    gp = np.asarray(L0_ffn_gate_proj, dtype=np.float32)
    up = np.asarray(L0_ffn_up_proj, dtype=np.float32)
    dp = np.asarray(L0_ffn_down_proj, dtype=np.float32)

    wqT = np.ascontiguousarray(wq.T)
    wkT = np.ascontiguousarray(wk.T)
    wv_c = np.ascontiguousarray(wv)
    gwT = np.ascontiguousarray(gw.T)
    ebb = np.ascontiguousarray(np.tile(eb[None, :], (128, TCH)))
    gpT = np.ascontiguousarray(gp.transpose(0, 2, 1))
    upT = np.ascontiguousarray(up.transpose(0, 2, 1))
    dpT = np.ascontiguousarray(dp.transpose(0, 2, 1))
    ident = np.eye(128, dtype=np.float32)
    identb = np.eye(128, dtype=ml_dtypes.bfloat16)
    epsvec = np.full((128, 1), EPS, dtype=np.float32)

    nc = build_nc()
    in_maps = []
    for core in range(NCORES):
        xt_c = np.ascontiguousarray(x[core])
        xf_c = np.ascontiguousarray(x[core].T)
        in_maps.append({
            "xt": xt_c, "xf": xf_c,
            "wqT": wqT, "wkT": wkT, "wv": wv_c,
            "gwT": gwT, "ebb": ebb,
            "gpT": gpT, "upT": upT, "dpT": dpT,
            "ident": ident, "identb": identb, "epsv": epsvec,
        })
    res = run_bass_kernel_spmd(nc, in_maps, core_ids=list(range(NCORES)))
    outs = [np.asarray(r["out"], dtype=np.float32) for r in res.results]
    return np.stack(outs, axis=0)


if __name__ == "__main__":
    nc = build_nc()
    print("build OK")





